# version 20
# baseline (speedup 1.0000x reference)
import math
import numpy as np
import ml_dtypes
import concourse.bass as bass
import concourse.mybir as mybir
from concourse.bass_utils import run_bass_kernel_spmd

F32 = mybir.dt.float32
BF16 = mybir.dt.bfloat16
I32 = mybir.dt.int32
AF = mybir.ActivationFunctionType
ALU = mybir.AluOpType
AX = mybir.AxisListType

L = 8192
DM = 1024
C = 128
NCH = L // C
NH = 16
HD = 64

LN8INV = float(np.log(np.float32(0.125)))
LN127 = float(np.log(np.float32(127.0)))
NLN127 = -LN127
ONE = 1.0
EPS5 = 1e-5
EPS8 = 1e-8
WMEANSCALE = float(1.0 / (1024.0 * 1024.0))

BF = ml_dtypes.bfloat16

ENGS = ("pe", "dve", "act", "pool", "sp")
PSUM_BANKS = frozenset(["A0", "A1", "B0", "B1", "C0", "C1", "D0", "D1"])
TINY = frozenset(
    ["c3", "c4", "c5", "rs16", "rstd", "ebt", "wst", "brow", "wbc"]
    + ["wacc%d" % j for j in range(8)]
    + ["l1c0_" + n for n in ("negm", "s2c", "rstd")]
    + ["l2c0_" + n for n in ("negm", "s2c", "rstd")]
)
FRESH_BIG = frozenset(["xbf", "gkh", "qtt", "kht", "kbt", "og0", "og1",
                       "aqbf", "scr2", "S_sb", "S_tmp", "S_bd", "gkf", "sgw",
                       "vbf", "qtf", "khf", "ogf", "aqf", "r1f", "x1f", "aqi"])


class Sched:

    def __init__(self):
        self.n = 0
        self.eng = []
        self.fn = []
        self.inc = []
        self.dur = []
        self.deps = []
        self.res = {}
        self.order = None

    def op(self, eng, fn, reads=(), writes=(), inc=1, dur=1000, tiny_safe=False, guard=False):
        oid = self.n
        self.n += 1
        is_dma = inc == 16
        deps = set()
        for r in reads:
            rec = self.res.get(r)
            if rec and rec["w"] is not None:
                deps.add(rec["w"])
            if rec and r in PSUM_BANKS and eng in ("dve", "act"):
                other = "act" if eng == "dve" else "dve"
                if other in rec["r"]:
                    deps.add(rec["r"][other])
        for r in writes:
            rec = self.res.get(r)
            if rec:
                if rec["w"] is not None:
                    deps.add(rec["w"])
                deps.update(rec["r"].values())
        tdeps = []
        fdeps = []
        if not tiny_safe:
            for r in reads:
                rec = self.res.get(r)
                if rec and rec["w"] is not None:
                    if r in TINY:
                        tdeps.append(rec["w"])
                    elif r in FRESH_BIG:
                        fdeps.append(rec["w"])
        delay = 0
        for r in reads:
            rec = self.res.setdefault(r, {"w": None, "r": {}})
            rec["r"][eng] = oid
        for r in writes:
            self.res[r] = {"w": oid, "r": {}}
        self.eng.append(eng)
        self.fn.append(fn)
        self.inc.append(inc)
        self.dur.append(dur)
        self.deps.append(deps)
        if not hasattr(self, "tinydeps"):
            self.tinydeps = []
            self.freshdeps = []
        self.tinydeps.append(tdeps)
        self.freshdeps.append(fdeps)

    def schedule(self):
        import heapq
        n = self.n
        nsucc = [[] for _ in range(n)]
        indeg = [0] * n
        for i in range(n):
            for d in self.deps[i]:
                nsucc[d].append(i)
                indeg[i] += 1
        cp = [0.0] * n
        for i in range(n - 1, -1, -1):
            m = max((cp[j] for j in nsucc[i]), default=0.0)
            cp[i] = self.dur[i] + m
        self.cp = cp
        fin = [0.0] * n
        start = [0.0] * n
        t_eng = {e: 0.0 for e in ENGS}
        ready = {e: [] for e in ENGS}
        sched_fin = [None] * n
        for i in range(n):
            if indeg[i] == 0:
                heapq.heappush(ready[self.eng[i]], i)
        out = {e: [] for e in ENGS}
        self.dispatch = []
        done = 0
        while done < n:
            best = None
            for e in ENGS:
                if not ready[e]:
                    continue
                cands = heapq.nsmallest(40, ready[e])
                for i in cands:
                    st = max(t_eng[e], max((fin[d] for d in self.deps[i]), default=0.0))
                    key = (st, i)
                    if best is None or key < best[0]:
                        best = (key, e, i)
            _, e, i = best
            ready[e].remove(i)
            heapq.heapify(ready[e])
            st = max(t_eng[e], max((fin[d] for d in self.deps[i]), default=0.0))
            start[i] = st
            fin[i] = st + self.dur[i]
            t_eng[e] = fin[i]
            out[e].append(i)
            self.dispatch.append(i)
            done += 1
            for j in nsucc[i]:
                indeg[j] -= 1
                if indeg[j] == 0:
                    heapq.heappush(ready[self.eng[j]], j)
        self.order = out
        self.start = start
        self.finv = fin
        self.makespan = max(t_eng.values())
        return out

def build_nc(nch=NCH, do_init_ffn=True, do_loop=True, op_limit=None, dump=False):
    nc = bass.Bass("TRN2", target_bir_lowering=False)

    x_d = nc.dram_tensor("x", [L, DM], F32, kind="ExternalInput").ap()
    xT_d = nc.dram_tensor("xT", [DM, L], F32, kind="ExternalInput").ap()
    wq_d = nc.dram_tensor("wq", [DM, DM], F32, kind="ExternalInput").ap()
    wk_d = nc.dram_tensor("wk", [DM, DM], F32, kind="ExternalInput").ap()
    wv_d = nc.dram_tensor("wv", [DM, DM], F32, kind="ExternalInput").ap()
    wg_d = nc.dram_tensor("wg", [DM, DM], F32, kind="ExternalInput").ap()
    wo_d = nc.dram_tensor("wo", [DM, DM], F32, kind="ExternalInput").ap()
    wg1_d = nc.dram_tensor("wg1", [DM, 16], F32, kind="ExternalInput").ap()
    wg2b_d = nc.dram_tensor("wg2b", [17, DM], F32, kind="ExternalInput").ap()
    wffnT_d = nc.dram_tensor("wffnT", [DM, DM], F32, kind="ExternalInput").ap()
    gnw_d = nc.dram_tensor("gnw", [128, DM], F32, kind="ExternalInput").ap()
    ui16_d = nc.dram_tensor("ui16", [128, 128], BF16, kind="ExternalInput").ap()
    sl16_d = nc.dram_tensor("sl16", [128, 128], BF16, kind="ExternalInput").ap()
    masku_d = nc.dram_tensor("masku", [128, 512], BF16, kind="ExternalInput").ap()
    ident_d = nc.dram_tensor("ident", [128, 128], BF16, kind="ExternalInput").ap()
    ones16_d = nc.dram_tensor("ones16", [128, 1], BF16, kind="ExternalInput").ap()
    onesrow_d = nc.dram_tensor("onesrow", [1, 128], F32, kind="ExternalInput").ap()
    cpool_d = nc.dram_tensor("cpool", [128, 8], F32, kind="ExternalInput").ap()
    ws_scr = nc.dram_tensor("ws_scr", [128, 8], F32).ap()
    bc_scr = nc.dram_tensor("bc_scr", [1, 4], F32).ap()
    out_d = nc.dram_tensor("out", [L, DM], F32, kind="ExternalOutput").ap()

    A = nc.alloc_sbuf_tensor
    w_q = A("w_q", [128, 8192], BF16)
    w_k = A("w_k", [128, 8192], BF16)
    w_v = A("w_v", [128, 8192], BF16)
    w_g = A("w_g", [128, 8192], BF16)
    w_o = A("w_o", [128, 8192], BF16)
    w_fT = A("w_fT", [128, 8192], BF16)
    w_g1 = A("w_g1", [128, 128], BF16)
    w_g2b = A("w_g2b", [17, 1024], BF16)
    ui16 = A("sb_ui16", [128, 128], BF16)
    sl16 = A("sb_sl16", [128, 128], BF16)
    masku = A("sb_masku", [128, 512], BF16)
    ident = A("sb_ident", [128, 128], BF16)
    ones16 = A("sb_ones16", [128, 1], BF16)
    onesrow = A("sb_onesrow", [1, 128], F32)
    onescol = A("onescol", [128, 1], F32)
    cpool = A("sb_cpool", [128, 8], F32)
    gnw = A("sb_gnw", [128, DM], F32)

    xtok = [A("xtok0", [128, DM], F32), A("xtok1", [128, DM], F32)]
    outb = [A("outb0", [128, DM], F32), A("outb1", [128, DM], F32)]
    xfeat = [A("xfeat0", [128, DM], BF16), A("xfeat1", [128, DM], BF16)]
    xbf = A("xbf", [128, DM], BF16)
    t1b = A("t1b", [32, 128], BF16)

    scr1 = A("scr1", [128, DM], F32)
    scr2 = A("scr2", [128, DM], F32)
    gkf = A("gkf", [128, DM], F32)
    expB = A("expB", [128, DM], F32)
    expnB = A("expnB", [128, DM], F32)
    expBrev = A("expBrev", [128, DM], F32)
    sq = A("sq", [128, DM], F32)
    r1f = A("r1f", [128, DM], F32)
    x1f = A("x1f", [128, DM], F32)

    gkh = A("gkh", [128, DM], BF16)
    qtt = A("qtt", [128, DM], BF16)
    kht = A("kht", [128, DM], BF16)
    kbt = A("kbt", [128, DM], BF16)
    qtf = A("qtf", [128, DM], BF16)
    khf = A("khf", [128, DM], BF16)
    vbf = A("vbf", [128, DM], BF16)
    og = [A("og0", [128, DM], BF16), A("og1", [128, DM], BF16)]
    ogf = A("ogf", [128, DM], BF16)
    aqbf = A("aqbf", [128, DM], BF16)
    aqf = A("aqf", [128, DM], BF16)
    aqi = A("aqi", [128, DM], I32)
    pm = [A("pm0", [128, 512], BF16), A("pm1", [128, 512], BF16)]

    S_sb = A("S_sb", [128, 512], F32)
    S_tmp = A("S_tmp", [128, 512], F32)
    S_bd = A("S_bd", [128, DM], BF16)
    ebt = A("ebt", [128, 8], F32)

    rs16 = A("rs16", [128, 16], F32)
    rstd = A("rstd", [128, 16], F32)
    cst1 = A("cst1", [128, 16], F32)
    cst2 = A("cst2", [128, 16], F32)
    c3 = A("c3", [128, 1], F32)
    c4 = A("c4", [128, 1], F32)
    c5 = A("c5", [128, 1], F32)

    wst = A("wst", [1, 1], F32)
    wrow = A("wrow", [1, 1024], F32)
    brow = A("brow", [1, 4], F32)
    wbc = A("wbc", [128, 4], F32)
    wacc = A("wacc", [128, 8], F32)
    wsum = A("wsum", [128, 1], F32)

    nc.const_aps.register(F32, ONE, cpool[:, 0:1])
    nc.const_aps.register(F32, EPS5, cpool[:, 1:2])
    nc.const_aps.register(F32, EPS8, cpool[:, 2:3])
    nc.const_aps.register(F32, LN8INV, cpool[:, 3:4])
    nc.const_aps.register(F32, LN127, cpool[:, 4:5])
    nc.const_aps.register(F32, NLN127, cpool[:, 5:6])

    pA = nc.alloc_psum_tensor("pA", [128, 1024], F32)
    pB = nc.alloc_psum_tensor("pB", [128, 1024], F32)
    pC = nc.alloc_psum_tensor("pC", [128, 1024], F32)
    pD = nc.alloc_psum_tensor("pD", [128, 1024], F32)
    pB16 = pB[:].bitcast(BF16)

    sems = {e: nc.alloc_semaphore("sem_" + e) for e in ENGS}
    sch = Sched()

    def proj_mm(psum, lhs_src, w_sb, tag):
        if not callable(lhs_src):
            ap = lhs_src
            lhs_src = lambda k: ap[:, 128 * k:128 * (k + 1)]
        def f(h):
            last = None
            for half in range(2):
                o = psum[:, 512 * half:512 * (half + 1)]
                for k in range(8):
                    last = h.matmul(out=o, lhsT=lhs_src(k),
                                    rhs=w_sb[:, 1024 * k + 512 * half:1024 * k + 512 * half + 512],
                                    start=(k == 0), stop=(k == 7))
            return last
        return f

    def transpose8(psum16, coloff, src, n=8):
        def f(h):
            last = None
            for j in range(n):
                last = h.transpose(out=psum16[:, coloff + 128 * j:coloff + 128 * (j + 1)],
                                   in_=src[:, 128 * j:128 * (j + 1)], identity=ident[:])
            return last
        return f

    def v3(ap, g=16):
        return ap.rearrange("p (g f) -> p g f", g=g)

    def bcast(ap2d, g, f):
        return ap2d.rearrange("p (g o) -> p g o", o=1).broadcast_to([128, g, f])

    def emit_ln(src, srcname, junk, junkname, sqb, sqname, cst, outap, outname, pref):
        negm, s2c, rstd = cst[:, 1:2], cst[:, 2:3], cst[:, 3:4]
        P = pref
        sch.op("act", lambda h: h.activation(out=junk, in_=src, func=AF.Identity,
                                             scale=-1.0 / 1024.0, accum_out=negm),
               reads=[srcname], writes=[junkname, P + "negm"], dur=700)
        sch.op("act", lambda h: h.activation(out=sqb, in_=src, func=AF.Square, bias=negm,
                                             accum_out=s2c),
               reads=[srcname, P + "negm"], writes=[sqname, P + "s2c"], dur=700)
        sch.op("act", lambda h: h.activation(out=rstd, in_=s2c, func=AF.Ln,
                                             scale=1.0 / 1024.0, bias=EPS5),
               reads=[P + "s2c", "cst"], writes=[P + "rstd"], dur=250)
        sch.op("act", lambda h: h.activation(out=rstd, in_=rstd, func=AF.Exp, scale=-0.5),
               reads=[P + "rstd"], writes=[P + "rstd"], dur=250, tiny_safe=True)
        sch.op("dve", lambda h: h.tensor_scalar(out=outap, in0=src, scalar1=negm,
                                                scalar2=rstd, op0=ALU.add, op1=ALU.mult),
               reads=[srcname, P + "negm", P + "rstd"], writes=[outname], dur=700)

    for dst, src in [(ui16, ui16_d), (sl16, sl16_d), (masku, masku_d),
                     (ident, ident_d), (ones16, ones16_d),
                     (onesrow[0:1, :], onesrow_d), (cpool, cpool_d), (gnw, gnw_d)]:
        sch.op("sp", (lambda d, s: (lambda h: h.dma_start(out=d, in_=s)))(
            dst[:] if hasattr(dst, "ap") else dst, src),
            writes=["cst"], inc=16)
    for name, dst, src, a in [("w_q", w_q, wq_d, 8), ("w_k", w_k, wk_d, 8),
                              ("w_v", w_v, wv_d, 8), ("w_g", w_g, wg_d, 8),
                              ("w_o", w_o, wo_d, 8)]:
        dview = dst[:].rearrange("p (a n) -> p a n", a=a)
        sview = src.rearrange("(a p) n -> p a n", p=128)
        sch.op("pool", (lambda d, s: (lambda h: h.dma_start(out=d, in_=s)))(dview, sview),
               writes=[name], inc=16)
    sch.op("pool", lambda h: h.dma_start(
        out=w_g1[:].rearrange("p (a n) -> p a n", n=16),
        in_=wg1_d.rearrange("(a p) n -> p a n", p=128)), writes=["w_g1"], inc=16)
    sch.op("pool", lambda h: h.dma_start(out=w_g2b[0:17, :], in_=wg2b_d),
           writes=["w_g2b"], inc=16)

    sch.op("dve", lambda h: h.memset(onescol[:], 1.0), writes=["onescol"])
    sch.op("dve", lambda h: h.memset(t1b[:], 1.0), writes=["t1b_ones"])
    sch.op("dve", lambda h: h.memset(S_sb[:], 0.0), writes=["S_sb"])
    sch.op("dve", lambda h: h.memset(S_bd[:], 0.0), writes=["S_bd"])

    if not do_init_ffn:
        wfT_all = []

    for j in range(8 if do_init_ffn else 0):
        buf, rname = (scr1, "scr1") if j % 2 == 0 else (scr2, "scr2")
        sch.op("sp", (lambda b, jj: (lambda h: h.dma_start(out=b[:], in_=wffnT_d[128 * jj:128 * (jj + 1), :])))(buf, j),
               writes=[rname], inc=16)
        sch.op("dve", (lambda b, jj: (lambda h: h.reduce_sum(
            out=wacc[:, jj:jj + 1], in_=b[:], axis=AX.X, apply_absolute_value=True)))(buf, j),
            reads=[rname], writes=["wacc%d" % j])
    if do_init_ffn:
      sch.op("act", lambda h: h.copy(out=ebt[:], in_=wacc[:]),
           reads=["wacc%d" % j for j in range(8)], writes=["ebt"])
    if do_init_ffn:
      def f_wsscr(h):
          h.nop(cycle_cnt=4000)
          return h.dma_start(out=ws_scr, in_=ebt[:])
      sch.op("sp", f_wsscr, reads=["ebt"], writes=["ws_scr"], inc=16)
      sch.op("sp", lambda h: h.dma_start(
          out=wrow[0:1, :], in_=ws_scr.rearrange("(o a) b -> o (a b)", o=1)),
          reads=["ws_scr"], writes=["wrow"], inc=16)
      sch.op("act", lambda h: h.activation(out=wrow[0:1, :], in_=wrow[0:1, :],
                                           func=AF.Identity, accum_out=wst[0:1, :]),
             reads=["wrow"], writes=["wrow", "wst"])
      sch.op("act", lambda h: h.activation(out=wst[0:1, :], in_=wst[0:1, :], func=AF.Ln,
                                           scale=WMEANSCALE, bias=EPS8),
             reads=["wst", "cst"], writes=["wst"])
      sch.op("act", lambda h: h.activation(out=brow[0:1, 0:1], in_=wst[0:1, :],
                                           func=AF.Exp, scale=-1.0),
             reads=["wst"], writes=["brow"])
      sch.op("act", lambda h: h.activation(out=brow[0:1, 1:2], in_=wst[0:1, :],
                                           func=AF.Exp, bias=NLN127),
             reads=["wst", "cst"], writes=["brow"])
      sch.op("act", lambda h: h.activation(out=brow[0:1, 2:3], in_=brow[0:1, 1:2],
                                           func=AF.Identity, scale=EPS8),
             reads=["brow"], writes=["brow"])
      def f_bcscr(h):
          h.nop(cycle_cnt=4000)
          return h.dma_start(out=bc_scr, in_=brow[0:1, :])
      sch.op("sp", f_bcscr, reads=["brow"], writes=["bc_scr"], inc=16)
      sch.op("sp", lambda h: h.dma_start(out=wbc[:], in_=bc_scr.broadcast_to([128, 4])),
             reads=["bc_scr"], writes=["wbc"], inc=16)
    for j in ([0, 1, 2, 3, 4, 5, 6, 7, 0] if do_init_ffn else []):
        sch.op("sp", (lambda jj: (lambda h: h.dma_start(out=scr1[:], in_=wffnT_d[128 * jj:128 * (jj + 1), :])))(j),
               writes=["scr1"], inc=16)
        sch.op("dve", lambda h: h.tensor_scalar(out=aqi[:], in0=scr1[:], scalar1=wbc[:, 0:1],
                                                scalar2=None, op0=ALU.mult),
               reads=["scr1", "wbc"], writes=["aqi"])
        def f_clip(h):
            h.nop(cycle_cnt=4000)
            return h.tensor_scalar(out=scr2[:], in0=aqi[:], scalar1=1.0,
                                   scalar2=-1.0, op0=ALU.min, op1=ALU.max)
        sch.op("dve", f_clip, reads=["aqi"], writes=["scr2"])
        def f_wft(h, jj=j):
            h.nop(cycle_cnt=4000)
            return h.copy(out=w_fT[:, 1024 * jj:1024 * (jj + 1)], in_=scr2[:])
        sch.op("act", f_wft, reads=["scr2"], writes=["w_fT%d" % j])

    if do_init_ffn:
        wfT_all = ["w_fT%d" % j for j in range(8)]

    init_gate = (["w_fT%d" % j for j in range(8)] + ["wbc", "onescol", "t1b_ones",
                 "S_sb", "S_bd", "cst"] + list(wfT_all)) if do_init_ffn else ["cst"]
    for e in ("pe", "dve", "act", "pool"):
        sch.op(e, (lambda: (lambda h: h.nop(cycle_cnt=8000)))(),
               reads=init_gate, writes=["bar_" + e], dur=7000)
    init_gate = init_gate + ["bar_pe", "bar_dve", "bar_act", "bar_pool"]
    for c0 in range(min(2, nch if do_loop else 0)):
        def f_prime(h, cc=c0):
            h.nop(cycle_cnt=4000)
            return h.dma_start(out=xtok[cc % 2][:], in_=x_d[128 * cc:128 * (cc + 1), :])
        sch.op("sp", f_prime, reads=init_gate,
               writes=["xtok%d" % (c0 % 2)], inc=16)

    sch.limit = op_limit
    pC16 = pC[:].bitcast(BF16)
    pD16 = pD[:].bitcast(BF16)

    def emit_head_mid(c):
        guard = c < 3
        _schop = sch.op
        def sopg(*a, **kw):
            kw.setdefault("guard", guard)
            return _schop(*a, **kw)
        sch_op_saved = sch.op
        p = c % 2
        XT = "xtok%d" % p
        XF = "xfeat%d" % p
        OG = "og%d" % p
        xfb = xfeat[p]

        def xf_slice(k):
            return xfb[:, 128 * k:128 * (k + 1)]

        sopg("pool", (lambda pp: (lambda h: h.tensor_copy(out=xbf[:], in_=xtok[pp][:])))(p),
               reads=[XT], writes=["xbf"], dur=1200)
        sopg("pe", transpose8(pC16, 0, xbf[:]), reads=["xbf", "cst"], writes=["C0"], dur=2600)
        sopg("act", (lambda pp: (lambda h: h.copy(out=xfeat[pp][:], in_=pC16[:, 0:1024])))(p),
               reads=["C0"], writes=[XF], dur=420)

        def f_t1(h):
            last = None
            for k in range(8):
                last = h.matmul(out=pC[0:16, 0:128], lhsT=w_g1[:, 16 * k:16 * (k + 1)],
                                rhs=xf_slice(k),
                                start=(k == 0), stop=(k == 7))
            return last
        sopg("pe", f_t1, reads=[XF, "w_g1"], writes=["C0"], dur=800)
        sopg("act", lambda h: h.copy(out=t1b[0:16, :], in_=pC[0:16, 0:128]),
               reads=["C0"], writes=["t1b"], dur=300)

        def f_z(h):
            last = None
            for half in range(2):
                last = h.matmul(out=pC[:, 512 * half:512 * (half + 1)], lhsT=t1b[0:17, :],
                                rhs=w_g2b[0:17, 512 * half:512 * (half + 1)],
                                start=True, stop=True)
            return last
        sopg("pe", f_z, reads=["t1b", "t1b_ones", "w_g2b"], writes=["C0", "C1"], dur=500)

        sopg("act", lambda h: h.activation(out=scr1[:], in_=pC[:], func=AF.Abs),
               reads=["C0", "C1"], writes=["scr1"], dur=620)
        sopg("act", lambda h: h.activation(out=scr2[:], in_=scr1[:], func=AF.Exp, scale=-1.0),
               reads=["scr1"], writes=["scr2"], dur=620)
        sopg("act", lambda h: h.activation(out=scr1[:], in_=scr2[:], func=AF.Ln, bias=ONE),
               reads=["scr2", "cst"], writes=["scr1"], dur=620)
        sopg("dve", lambda h: h.tensor_scalar(out=scr2[:], in0=pC[:], scalar1=0.0,
                                                scalar2=None, op0=ALU.min),
               reads=["C0", "C1"], writes=["scr2"], dur=660)
        sopg("pool", lambda h: h.tensor_tensor(out=gkh[:], in0=scr2[:], in1=scr1[:], op=ALU.subtract),
               reads=["scr1", "scr2"], writes=["gkh"], dur=2500)

        def f_btot(h):
            last = None
            for j in range(8):
                last = h.matmul(out=pD[:, j:j + 1], lhsT=gkh[:, 128 * j:128 * (j + 1)],
                                rhs=ones16[:], start=True, stop=True)
            return last
        sopg("pe", f_btot, reads=["gkh", "cst"], writes=["D0"], dur=700)
        sopg("act", lambda h: h.activation(out=ebt[:], in_=pD[:, 0:8], func=AF.Exp),
               reads=["D0"], writes=["ebt"], dur=250)

        def f_B(h):
            last = None
            for half in range(2):
                last = h.matmul(out=pD[:, 512 * half:512 * (half + 1)], lhsT=ui16[:],
                                rhs=gkh[:, 512 * half:512 * half + 512], start=True, stop=True)
            return last
        sopg("pe", f_B, reads=["gkh", "cst"], writes=["D0", "D1"], dur=550)

        def f_Brev(h):
            last = None
            for half in range(2):
                last = h.matmul(out=pC[:, 512 * half:512 * (half + 1)], lhsT=sl16[:],
                                rhs=gkh[:, 512 * half:512 * half + 512], start=True, stop=True)
            return last
        sopg("pe", f_Brev, reads=["gkh", "cst"], writes=["C0", "C1"], dur=550)

        sopg("act", lambda h: h.activation(out=expB[:], in_=pD[:], func=AF.Exp, bias=LN8INV),
               reads=["D0", "D1", "cst"], writes=["expB"], dur=620)
        sopg("act", lambda h: h.activation(out=expnB[:], in_=pD[:], func=AF.Exp, scale=-1.0),
               reads=["D0", "D1"], writes=["expnB"], dur=620)
        sopg("act", lambda h: h.activation(out=expBrev[:], in_=pC[:], func=AF.Exp),
               reads=["C0", "C1"], writes=["expBrev"], dur=620)

        sopg("pe", proj_mm(pC, xf_slice, w_g, "g"), reads=[XF, "w_g"], writes=["C0", "C1"], dur=3900)
        sopg("act", lambda h: h.activation(out=scr1[:], in_=pC[:], func=AF.Abs),
               reads=["C0", "C1"], writes=["scr1"], dur=620)
        sopg("act", lambda h: h.activation(out=scr1[:], in_=scr1[:], func=AF.Exp, scale=-1.0),
               reads=["scr1"], writes=["scr1"], dur=620)
        sopg("act", lambda h: h.activation(out=scr1[:], in_=scr1[:], func=AF.Ln, bias=ONE),
               reads=["scr1", "cst"], writes=["scr1"], dur=620)
        sopg("dve", lambda h: h.tensor_scalar(out=scr2[:], in0=pC[:], scalar1=0.0,
                                                scalar2=None, op0=ALU.min),
               reads=["C0", "C1"], writes=["scr2"], dur=660)
        sopg("dve", lambda h: h.tensor_tensor(out=scr2[:], in0=scr2[:], in1=scr1[:], op=ALU.subtract),
               reads=["scr1", "scr2"], writes=["scr2"], dur=1130)
        sopg("act", lambda h: h.activation(out=gkf[:], in_=scr2[:], func=AF.Exp),
               reads=["scr2"], writes=["gkf"], dur=620)
        sopg("dve", lambda h: h.scalar_tensor_tensor(out=scr1[:], in0=pC[:], scalar=1.0,
                                                       in1=gkf[:], op0=ALU.mult, op1=ALU.mult),
               reads=["C0", "C1", "gkf"], writes=["scr1"], dur=1200)
        sopg("pool", lambda h: h.tensor_tensor(out=scr2[:], in0=scr1[:], in1=gnw[:], op=ALU.mult),
               reads=["scr1", "cst"], writes=["scr2"], dur=2800)

        sopg("pe", proj_mm(pD, xf_slice, w_q, "q"), reads=[XF, "w_q"], writes=["D0", "D1"], dur=3900)
        sopg("dve", lambda h: h.tensor_tensor(out=qtt[:], in0=pD[:], in1=expB[:], op=ALU.mult),
               reads=["D0", "D1", "expB"], writes=["qtt"], dur=1200)
        sopg("pe", proj_mm(pD, xf_slice, w_k, "k"), reads=[XF, "w_k"], writes=["D0", "D1"], dur=3900)
        sopg("dve", lambda h: h.tensor_tensor(out=kht[:], in0=pD[:], in1=expnB[:], op=ALU.mult),
               reads=["D0", "D1", "expnB"], writes=["kht"], dur=1200)
        sopg("dve", lambda h: h.tensor_tensor(out=kbt[:], in0=pD[:], in1=expBrev[:], op=ALU.mult),
               reads=["D0", "D1", "expBrev"], writes=["kbt"], dur=1200)

        sopg("pe", transpose8(pD16, 0, qtt[:]), reads=["qtt", "cst"], writes=["D0"], dur=2600)
        sopg("pe", transpose8(pD16, 1024, kht[:]), reads=["kht", "cst"], writes=["D1"], dur=2600)
        sopg("dve", lambda h: h.tensor_copy(out=qtf[:], in_=pD16[:, 0:1024]), reads=["D0"], writes=["qtf"], dur=420)
        sopg("dve", lambda h: h.tensor_copy(out=khf[:], in_=pD16[:, 1024:2048]), reads=["D1"], writes=["khf"], dur=420)

        sopg("pe", proj_mm(pD, xf_slice, w_v, "v"), reads=[XF, "w_v"], writes=["D0", "D1"], dur=3900)
        sopg("act", lambda h: h.copy(out=vbf[:], in_=pD[:]), reads=["D0", "D1"], writes=["vbf"], dur=620)

        for b8 in range(2):
            def f_P8(h, bb=b8):
                last = None
                for i in range(8):
                    hh = 8 * bb + i
                    b = 64 * (hh % 2)
                    j = hh // 2
                    last = h.matmul(out=pC[:, 512 * (hh % 2) + 128 * (j % 4):
                                            512 * (hh % 2) + 128 * (j % 4) + 128],
                                    lhsT=khf[b:b + 64, 128 * j:128 * (j + 1)],
                                    rhs=qtf[b:b + 64, 128 * j:128 * (j + 1)],
                                    start=True, stop=True)
                return last
            sopg("pe", f_P8, reads=["khf", "qtf"], writes=["C0", "C1"], dur=1100)
            sopg("dve", lambda h: h.tensor_tensor(
                out=pm[0][:], in0=pC[:, 0:512], in1=masku[:], op=ALU.mult),
                reads=["C0", "cst"], writes=["pm0"], dur=660)
            sopg("dve", lambda h: h.tensor_tensor(
                out=pm[1][:], in0=pC[:, 512:1024], in1=masku[:], op=ALU.mult),
                reads=["C1", "cst"], writes=["pm1"], dur=660)

            for jp in range(4 * b8, 4 * b8 + 4):
                def f_O(h, j=jp):
                    last = h.matmul(out=pD[:, 128 * j:128 * (j + 1)],
                                    lhsT=qtf[:, 128 * j:128 * (j + 1)],
                                    rhs=S_bd[:, 128 * j:128 * (j + 1)],
                                    start=True, stop=False)
                    for t in range(2):
                        hh = 2 * j + t
                        last = h.matmul(out=pD[:, 64 * hh:64 * (hh + 1)],
                                        lhsT=pm[t][:, 128 * (j % 4):128 * (j % 4) + 128],
                                        rhs=vbf[:, 64 * hh:64 * (hh + 1)],
                                        start=False, stop=(t == 1))
                    return last
                sopg("pe", f_O, reads=["qtf", "S_bd", "pm0", "pm1", "vbf"],
                       writes=["D%d" % (jp // 4)], dur=400)

        def f_S(h, h0, h1):
            last = None
            for hh in range(h0, h1):
                b = 64 * (hh % 2)
                j = hh // 2
                last = h.matmul(out=pC[b:b + 64, 64 * j:64 * (j + 1)],
                                lhsT=kbt[:, 64 * hh:64 * (hh + 1)],
                                rhs=vbf[:, 64 * hh:64 * (hh + 1)],
                                start=True, stop=True)
            return last
        sopg("pe", (lambda fS: (lambda h: fS(h, 0, 8)))(f_S), reads=["kbt", "vbf"], writes=["C0"], dur=900)
        sopg("pe", (lambda fS: (lambda h: fS(h, 8, 16)))(f_S), reads=["kbt", "vbf"], writes=["C0"], dur=900)

        sopg("dve", lambda h: h.tensor_tensor(out=v3(S_tmp[:], 8), in0=v3(S_sb[:], 8),
                                                in1=bcast(ebt[:], 8, 64), op=ALU.mult),
               reads=["S_sb", "ebt"], writes=["S_tmp"], dur=620)
        sopg("dve", lambda h: h.tensor_tensor(out=S_sb[:], in0=pC[:, 0:512], in1=S_tmp[:], op=ALU.add),
               reads=["C0", "S_tmp"], writes=["S_sb"], dur=660)
        sopg("act", lambda h: h.copy(
            out=S_bd[0:64, :].rearrange("p (j n) -> p j n", j=8)[:, :, 0:64],
            in_=v3(S_sb[0:64, :], 8)), reads=["S_sb"], writes=["S_bd"], dur=450)
        sopg("act", lambda h: h.copy(
            out=S_bd[64:128, :].rearrange("p (j n) -> p j n", j=8)[:, :, 64:128],
            in_=v3(S_sb[64:128, :], 8)), reads=["S_sb"], writes=["S_bd"], dur=450)

        sopg("act", lambda h: h.activation(out=sq[:], in_=pD[:], func=AF.Square),
               reads=["D0", "D1"], writes=["sq"], dur=620)
        sopg("dve", lambda h: h.reduce_sum(out=rs16[:], in_=v3(sq[:], 16), axis=AX.X),
               reads=["sq"], writes=["rs16"], dur=1130)
        sopg("act", lambda h: h.activation(out=rs16[:], in_=rs16[:], func=AF.Ln,
                                             scale=1.0 / 64.0, bias=EPS5),
               reads=["rs16", "cst"], writes=["rs16"], dur=250)
        sopg("act", lambda h: h.activation(out=rstd[:], in_=rs16[:], func=AF.Exp, scale=-0.5),
               reads=["rs16"], writes=["rstd"], dur=250, tiny_safe=True)
        sopg("dve", lambda h: h.tensor_tensor(out=v3(gkf[:], 16), in0=v3(pD[:], 16),
                                                in1=bcast(rstd[:], 16, 64), op=ALU.mult),
               reads=["D0", "D1", "rstd"], writes=["gkf"], dur=1200)
        sopg("dve", (lambda pp: (lambda h: h.tensor_tensor(
            out=og[pp][:], in0=gkf[:], in1=scr2[:], op=ALU.mult)))(p),
            reads=["gkf", "scr2"], writes=[OG], dur=1130)

    def emit_tail(c):
        guard = c < 3
        _schop = sch.op
        def sopg(*a, **kw):
            kw.setdefault("guard", guard)
            return _schop(*a, **kw)
        p = c % 2
        XT = "xtok%d" % p
        OG = "og%d" % p

        sopg("pe", transpose8(pB16, 0, og[p][:]), reads=[OG, "cst"], writes=["B0"], dur=2600)
        sopg("dve", lambda h: h.tensor_copy(out=ogf[:], in_=pB16[:, 0:1024]), reads=["B0"], writes=["ogf"], dur=420)
        sopg("pe", proj_mm(pA, ogf[:], w_o, "wo"), reads=["ogf", "w_o"], writes=["A0", "A1"], dur=3900)
        sopg("dve", (lambda pp: (lambda h: h.tensor_tensor(
            out=r1f[:], in0=pA[:], in1=xtok[pp][:], op=ALU.add)))(p),
            reads=["A0", "A1", XT], writes=["r1f"], dur=1200)

        if dump and c == 0:
            sopg("pool", lambda h: h.dma_start(out=out_d[4096:4224, :], in_=r1f[:]),
                   reads=["r1f"], writes=["dumpr1"], inc=16)
        if c + 2 < nch:
            sopg("sp", (lambda cc: (lambda h: h.dma_start(
                out=xtok[cc % 2][:], in_=x_d[128 * cc:128 * (cc + 1), :])))(c + 2),
                writes=[XT], inc=16, dur=3500)

        emit_ln(r1f[:], "r1f", sq[:], "sq", sq[:], "sq", cst1, x1f[:], "x1f", "l1c0_")

        sopg("dve", lambda h: h.reduce_max(out=c3[:], in_=x1f[:], axis=AX.X,
                                             apply_absolute_value=True),
               reads=["x1f"], writes=["c3"], dur=1130)
        sopg("act", lambda h: h.activation(out=c4[:], in_=c3[:], func=AF.Ln, bias=EPS8),
               reads=["c3", "cst"], writes=["c4"], dur=250)
        sopg("act", lambda h: h.activation(out=c4[:], in_=c4[:], func=AF.Exp, scale=-1.0, bias=LN127),
               reads=["c4", "cst"], writes=["c4"], dur=250, tiny_safe=True)
        sopg("act", lambda h: h.activation(out=c5[:], in_=c3[:], func=AF.Identity,
                                             scale=wbc[:, 1:2], bias=wbc[:, 2:3]),
               reads=["c3", "wbc"], writes=["c5"], dur=250)
        sopg("dve", lambda h: h.tensor_scalar(out=aqi[:], in0=x1f[:], scalar1=c4[:, 0:1],
                                                scalar2=None, op0=ALU.mult),
               reads=["x1f", "c4"], writes=["aqi"], dur=660)
        sopg("dve", lambda h: h.tensor_copy(out=aqbf[:], in_=aqi[:]), reads=["aqi"], writes=["aqbf"], dur=1130)
        sopg("pe", transpose8(pB16, 1024, aqbf[:]), reads=["aqbf", "cst"], writes=["B1"], dur=2600)
        sopg("dve", lambda h: h.tensor_copy(out=aqf[:], in_=pB16[:, 1024:2048]), reads=["B1"], writes=["aqf"], dur=420)
        sopg("pe", proj_mm(pA, aqf[:], w_fT, "ffn"), reads=["aqf"] + wfT_all, writes=["A0", "A1"], dur=3900)
        sopg("dve", lambda h: h.scalar_tensor_tensor(out=r1f[:], in0=pA[:], scalar=c5[:, 0:1],
                                                       in1=x1f[:], op0=ALU.mult, op1=ALU.add),
               reads=["A0", "A1", "c5", "x1f"], writes=["r1f"], dur=1200)

        emit_ln(r1f[:], "r1f", sq[:], "sq", sq[:], "sq", cst2,
                outb[p][:], "outb%d" % p, "l2c0_")
        def f_store(h, cc=c, pp=p):
            h.nop(cycle_cnt=2500)
            return h.dma_start(out=out_d[128 * cc:128 * (cc + 1), :], in_=outb[pp][:])
        sopg("sp", f_store, reads=["outb%d" % p], inc=16, dur=5500)

    for c in range(nch if do_loop else 0):
        emit_head_mid(c)
        if c >= 1:
            emit_tail(c - 1)
    if do_loop and nch > 0:
        emit_tail(nch - 1)

    if dump:
        sch.limit = None
        dumps = [("gkh", gkh, BF16), ("expB", expB, F32), ("expnB", expnB, F32),
                 ("expBrev", expBrev, F32), ("qtt", qtt, BF16), ("kht", kht, BF16),
                 ("kbt", kbt, BF16), ("vbf", vbf, BF16), ("qtf", qtf, BF16),
                 ("khf", khf, BF16), ("og", og[0], BF16), ("ogf", ogf, BF16),
                 ("r1f", r1f, F32), ("x1f", x1f, F32), ("aqbf", aqbf, BF16),
                 ("S_sb", S_sb, F32), ("ebt", ebt, F32),
                 ("pm0", pm[0], BF16), ("pm1", pm[1], BF16), ("scr2gate", scr2, F32),
                 ("wbc", wbc, F32), ("aqf", aqf, BF16),
                 ("wacc", wacc, F32), ("wsum", wsum, F32), ("cst1", cst1, F32),
                 ("cst2", cst2, F32)]
        for jj in range(8):
            sch.op("pool", (lambda j2: (lambda h: h.dma_start(
                out=out_d[4352 + 128 * j2:4480 + 128 * j2, :],
                in_=w_fT[:, 1024 * j2:1024 * (j2 + 1)])))(jj),
                reads=["w_fT%d" % jj], writes=["dumpwft%d" % jj], inc=16)
        for di, (rname, buf, dt) in enumerate(dumps):
            row = 128 * (8 + di)
            wdt = buf[:].shape[1]
            sch.op("pool", (lambda b, r, w: (lambda h: h.dma_start(
                out=out_d[r:r + 128, 0:w], in_=b[:],
                allow_slow_non_contiguous=True)))(buf, row, wdt),
                reads=[rname], writes=["dump%d" % di], inc=16)
        globals()["DUMP_ROWS"] = {rname: 128 * (8 + di) for di, (rname, _, _) in enumerate(dumps)}

    base_dur = list(sch.dur)
    order = sch.schedule()
    for _ in range(3):
        sch.delays = [0] * sch.n
        for i in range(sch.n):
            for d in sch.tinydeps[i]:
                if sch.start[i] - sch.finv[d] < 5000.0:
                    sch.delays[i] = 1500
            for d in sch.freshdeps[i]:
                if sch.start[i] - sch.finv[d] < 2500.0:
                    sch.delays[i] = max(sch.delays[i], 1500)
        sch.dur = [base_dur[i] + sch.delays[i] * 1.25 for i in range(sch.n)]
        order = sch.schedule()
    globals()["LAST_SCHED"] = sch
    extra_wait = [dict() for _ in range(sch.n)]
    count = {}
    csum = {e: 0 for e in ENGS}
    pos = {}
    for e in ENGS:
        for k, i in enumerate(order[e]):
            csum[e] += sch.inc[i]
            count[i] = csum[e]
            pos[i] = k
    with nc.Block() as blk:
        def make_runner(ename):
            def run(h):
                last = {e: 0 for e in ENGS}
                for i in order[ename]:
                    waits = {}
                    for d in sch.deps[i]:
                        de = sch.eng[d]
                        if de != ename or sch.inc[d] == 16 or sch.inc[i] == 16:
                            if count[d] > waits.get(de, 0):
                                waits[de] = count[d]
                    for de, dop in extra_wait[i].items():
                        if de != ename and count[dop] > waits.get(de, 0):
                            waits[de] = count[dop]
                        elif de == ename:
                            pass
                    for e2, cnt in waits.items():
                        if cnt > last[e2]:
                            h.wait_ge(sems[e2], cnt)
                            last[e2] = cnt
                    if sch.delays[i]:
                        h.nop(cycle_cnt=sch.delays[i])
                    inst = sch.fn[i](h)
                    inst.then_inc(sems[ename], sch.inc[i])
                if ename == "sp":
                    h.wait_ge(sems["sp"], csum["sp"])
            return run
        blk.tensor(make_runner("pe"))
        blk.vector(make_runner("dve"))
        blk.scalar(make_runner("act"))
        blk.gpsimd(make_runner("pool"))
        blk.sync(make_runner("sp"))

    return nc


_NC_CACHE = None


def _get_nc():
    global _NC_CACHE
    if _NC_CACHE is None:
        _NC_CACHE = build_nc()
    return _NC_CACHE


def kernel(L0_attn_x, L0_attn_Wq, L0_attn_Wk, L0_attn_Wv, L0_attn_Wg1, L0_attn_Wg2,
           L0_attn_bg2, L0_attn_Wg, L0_attn_g_norm_w, L0_attn_Wo, L0_ffn_weight):
    x = np.ascontiguousarray(np.asarray(L0_attn_x, np.float32))
    base = dict(
        wq=np.ascontiguousarray(np.asarray(L0_attn_Wq, np.float32)),
        wk=np.ascontiguousarray(np.asarray(L0_attn_Wk, np.float32)),
        wv=np.ascontiguousarray(np.asarray(L0_attn_Wv, np.float32)),
        wg=np.ascontiguousarray(np.asarray(L0_attn_Wg, np.float32)),
        wo=np.ascontiguousarray(np.asarray(L0_attn_Wo, np.float32)),
        wg1=np.ascontiguousarray(np.asarray(L0_attn_Wg1, np.float32)),
        wg2b=np.ascontiguousarray(np.concatenate(
            [np.asarray(L0_attn_Wg2, np.float32),
             np.asarray(L0_attn_bg2, np.float32)[None, :]], axis=0)),
        wffnT=np.ascontiguousarray(np.asarray(L0_ffn_weight, np.float32).T),
        gnw=np.ascontiguousarray(np.broadcast_to(
            np.tile(np.asarray(L0_attn_g_norm_w, np.float32), NH), (128, DM))),
        ui16=np.triu(np.full((128, 128), 1.0 / 16.0, np.float32)).astype(BF),
        sl16=np.tril(np.full((128, 128), 1.0 / 16.0, np.float32), k=-1).astype(BF),
        masku=np.tile(np.triu(np.ones((128, 128), np.float32)), (1, 4)).astype(BF),
        ident=np.eye(128, dtype=np.float32).astype(BF),
        ones16=np.full((128, 1), 1.0 / 16.0, np.float32).astype(BF),
        onesrow=np.ones((1, 128), np.float32),
        cpool=np.stack([np.full(128, v, np.float32) for v in
                        (ONE, EPS5, EPS8, LN8INV, LN127, NLN127, 0.0, 0.0)], axis=1),
    )
    in_maps = []
    for b in range(8):
        m = dict(base)
        m["x"] = np.ascontiguousarray(x[b])
        m["xT"] = np.ascontiguousarray(x[b].T)
        in_maps.append(m)
    nc = _get_nc()
    res = run_bass_kernel_spmd(nc, in_maps, core_ids=list(range(8)))
    return np.stack([r["out"] for r in res.results], axis=0)


# version 22
# speedup vs baseline: 1.0028x; 1.0028x over previous
import math
import numpy as np
import ml_dtypes
import concourse.bass as bass
import concourse.mybir as mybir
from concourse.bass_utils import run_bass_kernel_spmd

F32 = mybir.dt.float32
BF16 = mybir.dt.bfloat16
I32 = mybir.dt.int32
AF = mybir.ActivationFunctionType
ALU = mybir.AluOpType
AX = mybir.AxisListType

L = 8192
DM = 1024
C = 128
NCH = L // C
NH = 16
HD = 64

LN8INV = float(np.log(np.float32(0.125)))
LN127 = float(np.log(np.float32(127.0)))
NLN127 = -LN127
ONE = 1.0
EPS5 = 1e-5
EPS8 = 1e-8
WMEANSCALE = float(1.0 / (1024.0 * 1024.0))

BF = ml_dtypes.bfloat16

ENGS = ("pe", "dve", "act", "pool", "sp")
PSUM_BANKS = frozenset(["A0", "A1", "B0", "B1", "C0", "C1", "D0", "D1"])
TINY = frozenset(
    ["c3", "c4", "c5", "rs16", "rstd", "ebt", "wst", "brow", "wbc"]
    + ["wacc%d" % j for j in range(8)]
    + ["l1c0_" + n for n in ("negm", "s2c", "rstd")]
    + ["l2c0_" + n for n in ("negm", "s2c", "rstd")]
)
FRESH_BIG = frozenset(["xbf", "gkh", "qtt", "kht", "kbt", "og0", "og1",
                       "aqbf", "scr2", "S_sb", "S_tmp", "S_bd", "gkf", "sgw",
                       "vbf", "qtf", "khf", "ogf", "aqf", "r1f", "x1f", "aqi"])


class Sched:

    def __init__(self):
        self.n = 0
        self.eng = []
        self.fn = []
        self.inc = []
        self.dur = []
        self.deps = []
        self.res = {}
        self.order = None

    def op(self, eng, fn, reads=(), writes=(), inc=1, dur=1000, tiny_safe=False, guard=False):
        oid = self.n
        self.n += 1
        is_dma = inc == 16
        deps = set()
        for r in reads:
            rec = self.res.get(r)
            if rec and rec["w"] is not None:
                deps.add(rec["w"])
            if rec and r in PSUM_BANKS and eng in ("dve", "act"):
                other = "act" if eng == "dve" else "dve"
                if other in rec["r"]:
                    deps.add(rec["r"][other])
        for r in writes:
            rec = self.res.get(r)
            if rec:
                if rec["w"] is not None:
                    deps.add(rec["w"])
                deps.update(rec["r"].values())
        tdeps = []
        fdeps = []
        if not tiny_safe:
            for r in reads:
                rec = self.res.get(r)
                if rec and rec["w"] is not None:
                    if r in TINY:
                        tdeps.append(rec["w"])
                    elif r in FRESH_BIG:
                        fdeps.append(rec["w"])
        delay = 0
        for r in reads:
            rec = self.res.setdefault(r, {"w": None, "r": {}})
            rec["r"][eng] = oid
        for r in writes:
            self.res[r] = {"w": oid, "r": {}}
        self.eng.append(eng)
        self.fn.append(fn)
        self.inc.append(inc)
        self.dur.append(dur)
        self.deps.append(deps)
        if not hasattr(self, "tinydeps"):
            self.tinydeps = []
            self.freshdeps = []
        self.tinydeps.append(tdeps)
        self.freshdeps.append(fdeps)

    def schedule(self):
        import heapq
        n = self.n
        nsucc = [[] for _ in range(n)]
        indeg = [0] * n
        for i in range(n):
            for d in self.deps[i]:
                nsucc[d].append(i)
                indeg[i] += 1
        cp = [0.0] * n
        for i in range(n - 1, -1, -1):
            m = max((cp[j] for j in nsucc[i]), default=0.0)
            cp[i] = self.dur[i] + m
        self.cp = cp
        fin = [0.0] * n
        start = [0.0] * n
        t_eng = {e: 0.0 for e in ENGS}
        ready = {e: [] for e in ENGS}
        sched_fin = [None] * n
        for i in range(n):
            if indeg[i] == 0:
                heapq.heappush(ready[self.eng[i]], i)
        out = {e: [] for e in ENGS}
        self.dispatch = []
        done = 0
        while done < n:
            best = None
            for e in ENGS:
                if not ready[e]:
                    continue
                cands = heapq.nsmallest(40, ready[e])
                for i in cands:
                    st = max(t_eng[e], max((fin[d] for d in self.deps[i]), default=0.0))
                    key = (st, i)
                    if best is None or key < best[0]:
                        best = (key, e, i)
            _, e, i = best
            ready[e].remove(i)
            heapq.heapify(ready[e])
            st = max(t_eng[e], max((fin[d] for d in self.deps[i]), default=0.0))
            start[i] = st
            fin[i] = st + self.dur[i]
            t_eng[e] = fin[i]
            out[e].append(i)
            self.dispatch.append(i)
            done += 1
            for j in nsucc[i]:
                indeg[j] -= 1
                if indeg[j] == 0:
                    heapq.heappush(ready[self.eng[j]], j)
        self.order = out
        self.start = start
        self.finv = fin
        self.makespan = max(t_eng.values())
        return out

def build_nc(nch=NCH, do_init_ffn=True, do_loop=True, op_limit=None, dump=False):
    nc = bass.Bass("TRN2", target_bir_lowering=False)

    x_d = nc.dram_tensor("x", [L, DM], F32, kind="ExternalInput").ap()
    xT_d = nc.dram_tensor("xT", [DM, L], F32, kind="ExternalInput").ap()
    wq_d = nc.dram_tensor("wq", [DM, DM], F32, kind="ExternalInput").ap()
    wk_d = nc.dram_tensor("wk", [DM, DM], F32, kind="ExternalInput").ap()
    wv_d = nc.dram_tensor("wv", [DM, DM], F32, kind="ExternalInput").ap()
    wg_d = nc.dram_tensor("wg", [DM, DM], F32, kind="ExternalInput").ap()
    wo_d = nc.dram_tensor("wo", [DM, DM], F32, kind="ExternalInput").ap()
    wg1_d = nc.dram_tensor("wg1", [DM, 16], F32, kind="ExternalInput").ap()
    wg2b_d = nc.dram_tensor("wg2b", [17, DM], F32, kind="ExternalInput").ap()
    wffnT_d = nc.dram_tensor("wffnT", [DM, DM], F32, kind="ExternalInput").ap()
    gnw_d = nc.dram_tensor("gnw", [128, DM], F32, kind="ExternalInput").ap()
    ui16_d = nc.dram_tensor("ui16", [128, 128], BF16, kind="ExternalInput").ap()
    sl16_d = nc.dram_tensor("sl16", [128, 128], BF16, kind="ExternalInput").ap()
    masku_d = nc.dram_tensor("masku", [128, 512], BF16, kind="ExternalInput").ap()
    ident_d = nc.dram_tensor("ident", [128, 128], BF16, kind="ExternalInput").ap()
    ones16_d = nc.dram_tensor("ones16", [128, 1], BF16, kind="ExternalInput").ap()
    onesrow_d = nc.dram_tensor("onesrow", [1, 128], F32, kind="ExternalInput").ap()
    cpool_d = nc.dram_tensor("cpool", [128, 8], F32, kind="ExternalInput").ap()
    ws_scr = nc.dram_tensor("ws_scr", [128, 8], F32).ap()
    bc_scr = nc.dram_tensor("bc_scr", [1, 4], F32).ap()
    out_d = nc.dram_tensor("out", [L, DM], F32, kind="ExternalOutput").ap()

    A = nc.alloc_sbuf_tensor
    w_q = A("w_q", [128, 8192], BF16)
    w_k = A("w_k", [128, 8192], BF16)
    w_v = A("w_v", [128, 8192], BF16)
    w_g = A("w_g", [128, 8192], BF16)
    w_o = A("w_o", [128, 8192], BF16)
    w_fT = A("w_fT", [128, 8192], BF16)
    w_g1 = A("w_g1", [128, 128], BF16)
    w_g2b = A("w_g2b", [17, 1024], BF16)
    ui16 = A("sb_ui16", [128, 128], BF16)
    sl16 = A("sb_sl16", [128, 128], BF16)
    masku = A("sb_masku", [128, 512], BF16)
    ident = A("sb_ident", [128, 128], BF16)
    ones16 = A("sb_ones16", [128, 1], BF16)
    onesrow = A("sb_onesrow", [1, 128], F32)
    onescol = A("onescol", [128, 1], F32)
    cpool = A("sb_cpool", [128, 8], F32)
    gnw = A("sb_gnw", [128, DM], F32)

    xtok = [A("xtok0", [128, DM], F32), A("xtok1", [128, DM], F32)]
    outb = [A("outb0", [128, DM], F32), A("outb1", [128, DM], F32)]
    xfeat = [A("xfeat0", [128, DM], BF16), A("xfeat1", [128, DM], BF16)]
    xbf = A("xbf", [128, DM], BF16)
    t1b = A("t1b", [32, 128], BF16)

    scr1 = A("scr1", [128, DM], F32)
    scr2 = A("scr2", [128, DM], F32)
    gkf = A("gkf", [128, DM], F32)
    expB = A("expB", [128, DM], F32)
    expnB = A("expnB", [128, DM], F32)
    expBrev = A("expBrev", [128, DM], F32)
    sq = A("sq", [128, DM], F32)
    r1f = A("r1f", [128, DM], F32)
    x1f = A("x1f", [128, DM], F32)

    gkh = A("gkh", [128, DM], BF16)
    qtt = A("qtt", [128, DM], BF16)
    kht = A("kht", [128, DM], BF16)
    kbt = A("kbt", [128, DM], BF16)
    qtf = A("qtf", [128, DM], BF16)
    khf = A("khf", [128, DM], BF16)
    vbf = A("vbf", [128, DM], BF16)
    og = [A("og0", [128, DM], BF16), A("og1", [128, DM], BF16)]
    ogf = A("ogf", [128, DM], BF16)
    aqbf = A("aqbf", [128, DM], BF16)
    aqf = A("aqf", [128, DM], BF16)
    aqi = A("aqi", [128, DM], I32)
    pm = [A("pm0", [128, 512], BF16), A("pm1", [128, 512], BF16)]

    S_sb = A("S_sb", [128, 512], F32)
    S_tmp = A("S_tmp", [128, 512], F32)
    S_bd = A("S_bd", [128, DM], BF16)
    ebt = A("ebt", [128, 8], F32)

    rs16 = A("rs16", [128, 16], F32)
    rstd = A("rstd", [128, 16], F32)
    cst1 = A("cst1", [128, 16], F32)
    cst2 = A("cst2", [128, 16], F32)
    c3 = A("c3", [128, 1], F32)
    c4 = A("c4", [128, 1], F32)
    c5 = A("c5", [128, 1], F32)

    wst = A("wst", [1, 1], F32)
    wrow = A("wrow", [1, 1024], F32)
    brow = A("brow", [1, 4], F32)
    wbc = A("wbc", [128, 4], F32)
    wacc = A("wacc", [128, 8], F32)
    wsum = A("wsum", [128, 1], F32)

    nc.const_aps.register(F32, ONE, cpool[:, 0:1])
    nc.const_aps.register(F32, EPS5, cpool[:, 1:2])
    nc.const_aps.register(F32, EPS8, cpool[:, 2:3])
    nc.const_aps.register(F32, LN8INV, cpool[:, 3:4])
    nc.const_aps.register(F32, LN127, cpool[:, 4:5])
    nc.const_aps.register(F32, NLN127, cpool[:, 5:6])

    pA = nc.alloc_psum_tensor("pA", [128, 1024], F32)
    pB = nc.alloc_psum_tensor("pB", [128, 1024], F32)
    pC = nc.alloc_psum_tensor("pC", [128, 1024], F32)
    pD = nc.alloc_psum_tensor("pD", [128, 1024], F32)
    pB16 = pB[:].bitcast(BF16)

    sems = {e: nc.alloc_semaphore("sem_" + e) for e in ENGS}
    sch = Sched()

    def proj_mm(psum, lhs_src, w_sb, tag):
        if not callable(lhs_src):
            ap = lhs_src
            lhs_src = lambda k: ap[:, 128 * k:128 * (k + 1)]
        def f(h):
            last = None
            for half in range(2):
                o = psum[:, 512 * half:512 * (half + 1)]
                for k in range(8):
                    last = h.matmul(out=o, lhsT=lhs_src(k),
                                    rhs=w_sb[:, 1024 * k + 512 * half:1024 * k + 512 * half + 512],
                                    start=(k == 0), stop=(k == 7))
            return last
        return f

    def transpose8(psum16, coloff, src, n=8):
        def f(h):
            last = None
            for j in range(n):
                last = h.transpose(out=psum16[:, coloff + 128 * j:coloff + 128 * (j + 1)],
                                   in_=src[:, 128 * j:128 * (j + 1)], identity=ident[:])
            return last
        return f

    def v3(ap, g=16):
        return ap.rearrange("p (g f) -> p g f", g=g)

    def bcast(ap2d, g, f):
        return ap2d.rearrange("p (g o) -> p g o", o=1).broadcast_to([128, g, f])

    def emit_ln(src, srcname, junk, junkname, sqb, sqname, cst, outap, outname, pref):
        negm, s2c, rstd = cst[:, 1:2], cst[:, 2:3], cst[:, 3:4]
        P = pref
        sch.op("act", lambda h: h.activation(out=junk, in_=src, func=AF.Identity,
                                             scale=-1.0 / 1024.0, accum_out=negm),
               reads=[srcname], writes=[junkname, P + "negm"], dur=700)
        sch.op("act", lambda h: h.activation(out=sqb, in_=src, func=AF.Square, bias=negm,
                                             accum_out=s2c),
               reads=[srcname, P + "negm"], writes=[sqname, P + "s2c"], dur=700)
        sch.op("act", lambda h: h.activation(out=rstd, in_=s2c, func=AF.Ln,
                                             scale=1.0 / 1024.0, bias=EPS5),
               reads=[P + "s2c", "cst"], writes=[P + "rstd"], dur=250)
        sch.op("act", lambda h: h.activation(out=rstd, in_=rstd, func=AF.Exp, scale=-0.5),
               reads=[P + "rstd"], writes=[P + "rstd"], dur=250, tiny_safe=True)
        sch.op("dve", lambda h: h.tensor_scalar(out=outap, in0=src, scalar1=negm,
                                                scalar2=rstd, op0=ALU.add, op1=ALU.mult),
               reads=[srcname, P + "negm", P + "rstd"], writes=[outname], dur=700)

    for dst, src in [(ui16, ui16_d), (sl16, sl16_d), (masku, masku_d),
                     (ident, ident_d), (ones16, ones16_d),
                     (onesrow[0:1, :], onesrow_d), (cpool, cpool_d), (gnw, gnw_d)]:
        sch.op("sp", (lambda d, s: (lambda h: h.dma_start(out=d, in_=s)))(
            dst[:] if hasattr(dst, "ap") else dst, src),
            writes=["cst"], inc=16)
    for name, dst, src, a in [("w_q", w_q, wq_d, 8), ("w_k", w_k, wk_d, 8),
                              ("w_v", w_v, wv_d, 8), ("w_g", w_g, wg_d, 8),
                              ("w_o", w_o, wo_d, 8)]:
        dview = dst[:].rearrange("p (a n) -> p a n", a=a)
        sview = src.rearrange("(a p) n -> p a n", p=128)
        sch.op("pool", (lambda d, s: (lambda h: h.dma_start(out=d, in_=s)))(dview, sview),
               writes=[name], inc=16)
    sch.op("pool", lambda h: h.dma_start(
        out=w_g1[:].rearrange("p (a n) -> p a n", n=16),
        in_=wg1_d.rearrange("(a p) n -> p a n", p=128)), writes=["w_g1"], inc=16)
    sch.op("pool", lambda h: h.dma_start(out=w_g2b[0:17, :], in_=wg2b_d),
           writes=["w_g2b"], inc=16)

    sch.op("dve", lambda h: h.memset(onescol[:], 1.0), writes=["onescol"])
    sch.op("dve", lambda h: h.memset(t1b[:], 1.0), writes=["t1b_ones"])
    sch.op("dve", lambda h: h.memset(S_sb[:], 0.0), writes=["S_sb"])
    sch.op("dve", lambda h: h.memset(S_bd[:], 0.0), writes=["S_bd"])

    if not do_init_ffn:
        wfT_all = []

    for j in range(8 if do_init_ffn else 0):
        buf, rname = (scr1, "scr1") if j % 2 == 0 else (scr2, "scr2")
        sch.op("sp", (lambda b, jj: (lambda h: h.dma_start(out=b[:], in_=wffnT_d[128 * jj:128 * (jj + 1), :])))(buf, j),
               writes=[rname], inc=16)
        sch.op("dve", (lambda b, jj: (lambda h: h.reduce_sum(
            out=wacc[:, jj:jj + 1], in_=b[:], axis=AX.X, apply_absolute_value=True)))(buf, j),
            reads=[rname], writes=["wacc%d" % j])
    if do_init_ffn:
      sch.op("act", lambda h: h.copy(out=ebt[:], in_=wacc[:]),
           reads=["wacc%d" % j for j in range(8)], writes=["ebt"])
    if do_init_ffn:
      def f_wsscr(h):
          h.nop(cycle_cnt=4000)
          return h.dma_start(out=ws_scr, in_=ebt[:])
      sch.op("sp", f_wsscr, reads=["ebt"], writes=["ws_scr"], inc=16)
      sch.op("sp", lambda h: h.dma_start(
          out=wrow[0:1, :], in_=ws_scr.rearrange("(o a) b -> o (a b)", o=1)),
          reads=["ws_scr"], writes=["wrow"], inc=16)
      sch.op("act", lambda h: h.activation(out=wrow[0:1, :], in_=wrow[0:1, :],
                                           func=AF.Identity, accum_out=wst[0:1, :]),
             reads=["wrow"], writes=["wrow", "wst"])
      sch.op("act", lambda h: h.activation(out=wst[0:1, :], in_=wst[0:1, :], func=AF.Ln,
                                           scale=WMEANSCALE, bias=EPS8),
             reads=["wst", "cst"], writes=["wst"])
      sch.op("act", lambda h: h.activation(out=brow[0:1, 0:1], in_=wst[0:1, :],
                                           func=AF.Exp, scale=-1.0),
             reads=["wst"], writes=["brow"])
      sch.op("act", lambda h: h.activation(out=brow[0:1, 1:2], in_=wst[0:1, :],
                                           func=AF.Exp, bias=NLN127),
             reads=["wst", "cst"], writes=["brow"])
      sch.op("act", lambda h: h.activation(out=brow[0:1, 2:3], in_=brow[0:1, 1:2],
                                           func=AF.Identity, scale=EPS8),
             reads=["brow"], writes=["brow"])
      def f_bcscr(h):
          h.nop(cycle_cnt=4000)
          return h.dma_start(out=bc_scr, in_=brow[0:1, :])
      sch.op("sp", f_bcscr, reads=["brow"], writes=["bc_scr"], inc=16)
      sch.op("sp", lambda h: h.dma_start(out=wbc[:], in_=bc_scr.broadcast_to([128, 4])),
             reads=["bc_scr"], writes=["wbc"], inc=16)
    for j in ([0, 1, 2, 3, 4, 5, 6, 7, 0] if do_init_ffn else []):
        sch.op("sp", (lambda jj: (lambda h: h.dma_start(out=scr1[:], in_=wffnT_d[128 * jj:128 * (jj + 1), :])))(j),
               writes=["scr1"], inc=16)
        sch.op("dve", lambda h: h.tensor_scalar(out=aqi[:], in0=scr1[:], scalar1=wbc[:, 0:1],
                                                scalar2=None, op0=ALU.mult),
               reads=["scr1", "wbc"], writes=["aqi"])
        def f_clip(h):
            h.nop(cycle_cnt=4000)
            return h.tensor_scalar(out=scr2[:], in0=aqi[:], scalar1=1.0,
                                   scalar2=-1.0, op0=ALU.min, op1=ALU.max)
        sch.op("dve", f_clip, reads=["aqi"], writes=["scr2"])
        def f_wft(h, jj=j):
            h.nop(cycle_cnt=4000)
            return h.copy(out=w_fT[:, 1024 * jj:1024 * (jj + 1)], in_=scr2[:])
        sch.op("act", f_wft, reads=["scr2"], writes=["w_fT%d" % j])

    if do_init_ffn:
        wfT_all = ["w_fT%d" % j for j in range(8)]

    init_gate = (["w_fT%d" % j for j in range(8)] + ["wbc", "onescol", "t1b_ones",
                 "S_sb", "S_bd", "cst"] + list(wfT_all)) if do_init_ffn else ["cst"]
    for e in ("pe", "dve", "act", "pool"):
        sch.op(e, (lambda: (lambda h: h.nop(cycle_cnt=8000)))(),
               reads=init_gate, writes=["bar_" + e], dur=7000)
    init_gate = init_gate + ["bar_pe", "bar_dve", "bar_act", "bar_pool"]
    for c0 in range(min(2, nch if do_loop else 0)):
        def f_prime(h, cc=c0):
            h.nop(cycle_cnt=4000)
            return h.dma_start(out=xtok[cc % 2][:], in_=x_d[128 * cc:128 * (cc + 1), :])
        sch.op("sp", f_prime, reads=init_gate,
               writes=["xtok%d" % (c0 % 2)], inc=16)

    sch.limit = op_limit
    pC16 = pC[:].bitcast(BF16)
    pD16 = pD[:].bitcast(BF16)

    def emit_head_mid(c):
        guard = c < 3
        _schop = sch.op
        def sopg(*a, **kw):
            kw.setdefault("guard", guard)
            return _schop(*a, **kw)
        sch_op_saved = sch.op
        p = c % 2
        XT = "xtok%d" % p
        XF = "xfeat%d" % p
        OG = "og%d" % p
        xfb = xfeat[p]

        def xf_slice(k):
            return xfb[:, 128 * k:128 * (k + 1)]

        sopg("pool", (lambda pp: (lambda h: h.tensor_copy(out=xbf[:], in_=xtok[pp][:])))(p),
               reads=[XT], writes=["xbf"], dur=1200)
        sopg("pe", transpose8(pC16, 0, xbf[:]), reads=["xbf", "cst"], writes=["C0"], dur=2600)
        sopg("act", (lambda pp: (lambda h: h.copy(out=xfeat[pp][:], in_=pC16[:, 0:1024])))(p),
               reads=["C0"], writes=[XF], dur=420)

        def f_t1(h):
            last = None
            for k in range(8):
                last = h.matmul(out=pC[0:16, 0:128], lhsT=w_g1[:, 16 * k:16 * (k + 1)],
                                rhs=xf_slice(k),
                                start=(k == 0), stop=(k == 7))
            return last
        sopg("pe", f_t1, reads=[XF, "w_g1"], writes=["C0"], dur=800)
        sopg("act", lambda h: h.copy(out=t1b[0:16, :], in_=pC[0:16, 0:128]),
               reads=["C0"], writes=["t1b"], dur=300)

        def f_z(h):
            last = None
            for half in range(2):
                last = h.matmul(out=pC[:, 512 * half:512 * (half + 1)], lhsT=t1b[0:17, :],
                                rhs=w_g2b[0:17, 512 * half:512 * (half + 1)],
                                start=True, stop=True)
            return last
        sopg("pe", f_z, reads=["t1b", "t1b_ones", "w_g2b"], writes=["C0", "C1"], dur=500)

        sopg("act", lambda h: h.activation(out=scr1[:], in_=pC[:], func=AF.Abs),
               reads=["C0", "C1"], writes=["scr1"], dur=620)
        sopg("act", lambda h: h.activation(out=scr2[:], in_=scr1[:], func=AF.Exp, scale=-1.0),
               reads=["scr1"], writes=["scr2"], dur=620)
        sopg("act", lambda h: h.activation(out=scr1[:], in_=scr2[:], func=AF.Ln, bias=ONE),
               reads=["scr2", "cst"], writes=["scr1"], dur=620)
        sopg("dve", lambda h: h.tensor_scalar(out=scr2[:], in0=pC[:], scalar1=0.0,
                                                scalar2=None, op0=ALU.min),
               reads=["C0", "C1"], writes=["scr2"], dur=660)
        sopg("pool", lambda h: h.tensor_tensor(out=gkh[:], in0=scr2[:], in1=scr1[:], op=ALU.subtract),
               reads=["scr1", "scr2"], writes=["gkh"], dur=2500)

        def f_btot(h):
            last = None
            for j in range(8):
                last = h.matmul(out=pD[:, j:j + 1], lhsT=gkh[:, 128 * j:128 * (j + 1)],
                                rhs=ones16[:], start=True, stop=True)
            return last
        sopg("pe", f_btot, reads=["gkh", "cst"], writes=["D0"], dur=700)
        sopg("act", lambda h: h.activation(out=ebt[:], in_=pD[:, 0:8], func=AF.Exp),
               reads=["D0"], writes=["ebt"], dur=250)

        def f_B(h):
            last = None
            for half in range(2):
                last = h.matmul(out=pD[:, 512 * half:512 * (half + 1)], lhsT=ui16[:],
                                rhs=gkh[:, 512 * half:512 * half + 512], start=True, stop=True)
            return last
        sopg("pe", f_B, reads=["gkh", "cst"], writes=["D0", "D1"], dur=550)

        def f_Brev(h):
            last = None
            for half in range(2):
                last = h.matmul(out=pC[:, 512 * half:512 * (half + 1)], lhsT=sl16[:],
                                rhs=gkh[:, 512 * half:512 * half + 512], start=True, stop=True)
            return last
        sopg("pe", f_Brev, reads=["gkh", "cst"], writes=["C0", "C1"], dur=550)

        sopg("act", lambda h: h.activation(out=expB[:], in_=pD[:], func=AF.Exp, bias=LN8INV),
               reads=["D0", "D1", "cst"], writes=["expB"], dur=620)
        sopg("act", lambda h: h.activation(out=expnB[:], in_=pD[:], func=AF.Exp, scale=-1.0),
               reads=["D0", "D1"], writes=["expnB"], dur=620)
        sopg("act", lambda h: h.activation(out=expBrev[:], in_=pC[:], func=AF.Exp),
               reads=["C0", "C1"], writes=["expBrev"], dur=620)

        sopg("pe", proj_mm(pC, xf_slice, w_g, "g"), reads=[XF, "w_g"], writes=["C0", "C1"], dur=3900)
        sopg("act", lambda h: h.activation(out=scr1[:], in_=pC[:], func=AF.Abs),
               reads=["C0", "C1"], writes=["scr1"], dur=620)
        sopg("act", lambda h: h.activation(out=scr1[:], in_=scr1[:], func=AF.Exp, scale=-1.0),
               reads=["scr1"], writes=["scr1"], dur=620)
        sopg("act", lambda h: h.activation(out=scr1[:], in_=scr1[:], func=AF.Ln, bias=ONE),
               reads=["scr1", "cst"], writes=["scr1"], dur=620)
        sopg("dve", lambda h: h.tensor_scalar(out=scr2[:], in0=pC[:], scalar1=0.0,
                                                scalar2=None, op0=ALU.min),
               reads=["C0", "C1"], writes=["scr2"], dur=660)
        sopg("dve", lambda h: h.tensor_tensor(out=scr2[:], in0=scr2[:], in1=scr1[:], op=ALU.subtract),
               reads=["scr1", "scr2"], writes=["scr2"], dur=1130)
        sopg("act", lambda h: h.activation(out=gkf[:], in_=scr2[:], func=AF.Exp),
               reads=["scr2"], writes=["gkf"], dur=620)
        sopg("dve", lambda h: h.scalar_tensor_tensor(out=scr1[:], in0=pC[:], scalar=1.0,
                                                       in1=gkf[:], op0=ALU.mult, op1=ALU.mult),
               reads=["C0", "C1", "gkf"], writes=["scr1"], dur=1200)
        sopg("pool", lambda h: h.tensor_tensor(out=scr2[:], in0=scr1[:], in1=gnw[:], op=ALU.mult),
               reads=["scr1", "cst"], writes=["scr2"], dur=2800)

        sopg("pe", proj_mm(pD, xf_slice, w_q, "q"), reads=[XF, "w_q"], writes=["D0", "D1"], dur=3900)
        sopg("dve", lambda h: h.tensor_tensor(out=qtt[:], in0=pD[:], in1=expB[:], op=ALU.mult),
               reads=["D0", "D1", "expB"], writes=["qtt"], dur=1200)
        sopg("pe", proj_mm(pD, xf_slice, w_k, "k"), reads=[XF, "w_k"], writes=["D0", "D1"], dur=3900)
        sopg("dve", lambda h: h.tensor_tensor(out=kht[:], in0=pD[:], in1=expnB[:], op=ALU.mult),
               reads=["D0", "D1", "expnB"], writes=["kht"], dur=1200)
        sopg("dve", lambda h: h.tensor_tensor(out=kbt[:], in0=pD[:], in1=expBrev[:], op=ALU.mult),
               reads=["D0", "D1", "expBrev"], writes=["kbt"], dur=1200)

        sopg("pe", transpose8(pD16, 0, qtt[:]), reads=["qtt", "cst"], writes=["D0"], dur=2600)
        sopg("pe", transpose8(pD16, 1024, kht[:]), reads=["kht", "cst"], writes=["D1"], dur=2600)
        sopg("dve", lambda h: h.tensor_copy(out=qtf[:], in_=pD16[:, 0:1024]), reads=["D0"], writes=["qtf"], dur=420)
        sopg("dve", lambda h: h.tensor_copy(out=khf[:], in_=pD16[:, 1024:2048]), reads=["D1"], writes=["khf"], dur=420)

        sopg("pe", proj_mm(pD, xf_slice, w_v, "v"), reads=[XF, "w_v"], writes=["D0", "D1"], dur=3900)
        sopg("act", lambda h: h.copy(out=vbf[:], in_=pD[:]), reads=["D0", "D1"], writes=["vbf"], dur=620)

        for b8 in range(2):
            def f_P8(h, bb=b8):
                last = None
                for i in range(8):
                    hh = 8 * bb + i
                    b = 64 * (hh % 2)
                    j = hh // 2
                    last = h.matmul(out=pC[:, 512 * (hh % 2) + 128 * (j % 4):
                                            512 * (hh % 2) + 128 * (j % 4) + 128],
                                    lhsT=khf[b:b + 64, 128 * j:128 * (j + 1)],
                                    rhs=qtf[b:b + 64, 128 * j:128 * (j + 1)],
                                    start=True, stop=True)
                return last
            sopg("pe", f_P8, reads=["khf", "qtf"], writes=["C0", "C1"], dur=1100)
            sopg("dve", lambda h: h.tensor_tensor(
                out=pm[0][:], in0=pC[:, 0:512], in1=masku[:], op=ALU.mult),
                reads=["C0", "cst"], writes=["pm0"], dur=660)
            sopg("dve", lambda h: h.tensor_tensor(
                out=pm[1][:], in0=pC[:, 512:1024], in1=masku[:], op=ALU.mult),
                reads=["C1", "cst"], writes=["pm1"], dur=660)

            for jp in range(4 * b8, 4 * b8 + 4):
                def f_O(h, j=jp):
                    last = h.matmul(out=pD[:, 128 * j:128 * (j + 1)],
                                    lhsT=qtf[:, 128 * j:128 * (j + 1)],
                                    rhs=S_bd[:, 128 * j:128 * (j + 1)],
                                    start=True, stop=False)
                    for t in range(2):
                        hh = 2 * j + t
                        last = h.matmul(out=pD[:, 64 * hh:64 * (hh + 1)],
                                        lhsT=pm[t][:, 128 * (j % 4):128 * (j % 4) + 128],
                                        rhs=vbf[:, 64 * hh:64 * (hh + 1)],
                                        start=False, stop=(t == 1))
                    return last
                sopg("pe", f_O, reads=["qtf", "S_bd", "pm0", "pm1", "vbf"],
                       writes=["D%d" % (jp // 4)], dur=400)

        def f_S(h, h0, h1):
            last = None
            for hh in range(h0, h1):
                b = 64 * (hh % 2)
                j = hh // 2
                last = h.matmul(out=pC[b:b + 64, 64 * j:64 * (j + 1)],
                                lhsT=kbt[:, 64 * hh:64 * (hh + 1)],
                                rhs=vbf[:, 64 * hh:64 * (hh + 1)],
                                start=True, stop=True)
            return last
        sopg("pe", (lambda fS: (lambda h: fS(h, 0, 8)))(f_S), reads=["kbt", "vbf"], writes=["C0"], dur=900)
        sopg("pe", (lambda fS: (lambda h: fS(h, 8, 16)))(f_S), reads=["kbt", "vbf"], writes=["C0"], dur=900)

        sopg("dve", lambda h: h.tensor_tensor(out=v3(S_tmp[:], 8), in0=v3(S_sb[:], 8),
                                                in1=bcast(ebt[:], 8, 64), op=ALU.mult),
               reads=["S_sb", "ebt"], writes=["S_tmp"], dur=620)
        sopg("dve", lambda h: h.tensor_tensor(out=S_sb[:], in0=pC[:, 0:512], in1=S_tmp[:], op=ALU.add),
               reads=["C0", "S_tmp"], writes=["S_sb"], dur=660)
        sopg("act", lambda h: h.copy(
            out=S_bd[0:64, :].rearrange("p (j n) -> p j n", j=8)[:, :, 0:64],
            in_=v3(S_sb[0:64, :], 8)), reads=["S_sb"], writes=["S_bd"], dur=450)
        sopg("act", lambda h: h.copy(
            out=S_bd[64:128, :].rearrange("p (j n) -> p j n", j=8)[:, :, 64:128],
            in_=v3(S_sb[64:128, :], 8)), reads=["S_sb"], writes=["S_bd"], dur=450)

        sopg("act", lambda h: h.activation(out=sq[:], in_=pD[:], func=AF.Square),
               reads=["D0", "D1"], writes=["sq"], dur=620)
        sopg("dve", lambda h: h.reduce_sum(out=rs16[:], in_=v3(sq[:], 16), axis=AX.X),
               reads=["sq"], writes=["rs16"], dur=1130)
        sopg("act", lambda h: h.activation(out=rs16[:], in_=rs16[:], func=AF.Ln,
                                             scale=1.0 / 64.0, bias=EPS5),
               reads=["rs16", "cst"], writes=["rs16"], dur=250)
        sopg("act", lambda h: h.activation(out=rstd[:], in_=rs16[:], func=AF.Exp, scale=-0.5),
               reads=["rs16"], writes=["rstd"], dur=250, tiny_safe=True)
        sopg("dve", lambda h: h.tensor_tensor(out=v3(gkf[:], 16), in0=v3(pD[:], 16),
                                                in1=bcast(rstd[:], 16, 64), op=ALU.mult),
               reads=["D0", "D1", "rstd"], writes=["gkf"], dur=1200)
        sopg("dve", (lambda pp: (lambda h: h.tensor_tensor(
            out=og[pp][:], in0=gkf[:], in1=scr2[:], op=ALU.mult)))(p),
            reads=["gkf", "scr2"], writes=[OG], dur=1130)

    def emit_tail(c):
        guard = c < 3
        _schop = sch.op
        def sopg(*a, **kw):
            kw.setdefault("guard", guard)
            return _schop(*a, **kw)
        p = c % 2
        XT = "xtok%d" % p
        OG = "og%d" % p

        sopg("pe", transpose8(pB16, 0, og[p][:]), reads=[OG, "cst"], writes=["B0"], dur=2600)
        sopg("dve", lambda h: h.tensor_copy(out=ogf[:], in_=pB16[:, 0:1024]), reads=["B0"], writes=["ogf"], dur=420)
        sopg("pe", proj_mm(pA, ogf[:], w_o, "wo"), reads=["ogf", "w_o"], writes=["A0", "A1"], dur=3900)
        sopg("dve", (lambda pp: (lambda h: h.tensor_tensor(
            out=r1f[:], in0=pA[:], in1=xtok[pp][:], op=ALU.add)))(p),
            reads=["A0", "A1", XT], writes=["r1f"], dur=1200)

        if dump and c == 0:
            sopg("pool", lambda h: h.dma_start(out=out_d[4096:4224, :], in_=r1f[:]),
                   reads=["r1f"], writes=["dumpr1"], inc=16)
        if c + 2 < nch:
            sopg("sp", (lambda cc: (lambda h: h.dma_start(
                out=xtok[cc % 2][:], in_=x_d[128 * cc:128 * (cc + 1), :])))(c + 2),
                writes=[XT], inc=16, dur=3500)

        emit_ln(r1f[:], "r1f", sq[:], "sq", sq[:], "sq", cst1, x1f[:], "x1f", "l1c0_")

        sopg("dve", lambda h: h.reduce_max(out=c3[:], in_=x1f[:], axis=AX.X,
                                             apply_absolute_value=True),
               reads=["x1f"], writes=["c3"], dur=1130)
        sopg("act", lambda h: h.activation(out=c4[:], in_=c3[:], func=AF.Ln, bias=EPS8),
               reads=["c3", "cst"], writes=["c4"], dur=250)
        sopg("act", lambda h: h.activation(out=c4[:], in_=c4[:], func=AF.Exp, scale=-1.0, bias=LN127),
               reads=["c4", "cst"], writes=["c4"], dur=250, tiny_safe=True)
        sopg("act", lambda h: h.activation(out=c5[:], in_=c3[:], func=AF.Identity,
                                             scale=wbc[:, 1:2], bias=wbc[:, 2:3]),
               reads=["c3", "wbc"], writes=["c5"], dur=250)
        sopg("dve", lambda h: h.tensor_scalar(out=aqi[:], in0=x1f[:], scalar1=c4[:, 0:1],
                                                scalar2=None, op0=ALU.mult),
               reads=["x1f", "c4"], writes=["aqi"], dur=660)
        sopg("dve", lambda h: h.tensor_copy(out=aqbf[:], in_=aqi[:]), reads=["aqi"], writes=["aqbf"], dur=1130)
        sopg("pe", transpose8(pB16, 1024, aqbf[:]), reads=["aqbf", "cst"], writes=["B1"], dur=2600)
        sopg("dve", lambda h: h.tensor_copy(out=aqf[:], in_=pB16[:, 1024:2048]), reads=["B1"], writes=["aqf"], dur=420)
        sopg("pe", proj_mm(pA, aqf[:], w_fT, "ffn"), reads=["aqf"] + wfT_all, writes=["A0", "A1"], dur=3900)
        sopg("dve", lambda h: h.scalar_tensor_tensor(out=r1f[:], in0=pA[:], scalar=c5[:, 0:1],
                                                       in1=x1f[:], op0=ALU.mult, op1=ALU.add),
               reads=["A0", "A1", "c5", "x1f"], writes=["r1f"], dur=1200)

        emit_ln(r1f[:], "r1f", sq[:], "sq", sq[:], "sq", cst2,
                outb[p][:], "outb%d" % p, "l2c0_")
        def f_store(h, cc=c, pp=p):
            h.nop(cycle_cnt=2500)
            return h.dma_start(out=out_d[128 * cc:128 * (cc + 1), :], in_=outb[pp][:])
        sopg("sp", f_store, reads=["outb%d" % p], inc=16, dur=5500)

    for c in range(nch if do_loop else 0):
        emit_head_mid(c)
        if c >= 1:
            emit_tail(c - 1)
    if do_loop and nch > 0:
        emit_tail(nch - 1)

    if dump:
        sch.limit = None
        dumps = [("gkh", gkh, BF16), ("expB", expB, F32), ("expnB", expnB, F32),
                 ("expBrev", expBrev, F32), ("qtt", qtt, BF16), ("kht", kht, BF16),
                 ("kbt", kbt, BF16), ("vbf", vbf, BF16), ("qtf", qtf, BF16),
                 ("khf", khf, BF16), ("og", og[0], BF16), ("ogf", ogf, BF16),
                 ("r1f", r1f, F32), ("x1f", x1f, F32), ("aqbf", aqbf, BF16),
                 ("S_sb", S_sb, F32), ("ebt", ebt, F32),
                 ("pm0", pm[0], BF16), ("pm1", pm[1], BF16), ("scr2gate", scr2, F32),
                 ("wbc", wbc, F32), ("aqf", aqf, BF16),
                 ("wacc", wacc, F32), ("wsum", wsum, F32), ("cst1", cst1, F32),
                 ("cst2", cst2, F32)]
        for jj in range(8):
            sch.op("pool", (lambda j2: (lambda h: h.dma_start(
                out=out_d[4352 + 128 * j2:4480 + 128 * j2, :],
                in_=w_fT[:, 1024 * j2:1024 * (j2 + 1)])))(jj),
                reads=["w_fT%d" % jj], writes=["dumpwft%d" % jj], inc=16)
        for di, (rname, buf, dt) in enumerate(dumps):
            row = 128 * (8 + di)
            wdt = buf[:].shape[1]
            sch.op("pool", (lambda b, r, w: (lambda h: h.dma_start(
                out=out_d[r:r + 128, 0:w], in_=b[:],
                allow_slow_non_contiguous=True)))(buf, row, wdt),
                reads=[rname], writes=["dump%d" % di], inc=16)
        globals()["DUMP_ROWS"] = {rname: 128 * (8 + di) for di, (rname, _, _) in enumerate(dumps)}

    base_dur = list(sch.dur)
    order = sch.schedule()
    def place_protection():
        sch.delays = [0] * sch.n
        for i in range(sch.n):
            for d in sch.tinydeps[i]:
                if sch.start[i] - sch.finv[d] < 5000.0:
                    sch.delays[i] = 1500
            for d in sch.freshdeps[i]:
                if sch.start[i] - sch.finv[d] < 2500.0:
                    sch.delays[i] = max(sch.delays[i], 1500)
        return {}

    hoists = place_protection()
    for _ in range(2):
        sch.dur = [base_dur[i] + sch.delays[i] * 1.25 for i in range(sch.n)]
        order = sch.schedule()
        hoists = place_protection()
    globals()["LAST_SCHED"] = sch
    count = {}
    csum = {e: 0 for e in ENGS}
    pos = {}
    for e in ENGS:
        for k, i in enumerate(order[e]):
            csum[e] += sch.inc[i]
            count[i] = csum[e]
            pos[i] = k
    with nc.Block() as blk:
        def make_runner(ename):
            def run(h):
                last = {e: 0 for e in ENGS}
                for k, i in enumerate(order[ename]):
                    hw = hoists.get((ename, k))
                    if hw:
                        for de, dop in hw.items():
                            if count[dop] > last[de]:
                                h.wait_ge(sems[de], count[dop])
                                last[de] = count[dop]
                    waits = {}
                    for d in sch.deps[i]:
                        de = sch.eng[d]
                        if de != ename or sch.inc[d] == 16 or sch.inc[i] == 16:
                            if count[d] > waits.get(de, 0):
                                waits[de] = count[d]
                    for e2, cnt in waits.items():
                        if cnt > last[e2]:
                            h.wait_ge(sems[e2], cnt)
                            last[e2] = cnt
                    if sch.delays[i]:
                        h.nop(cycle_cnt=sch.delays[i])
                    inst = sch.fn[i](h)
                    inst.then_inc(sems[ename], sch.inc[i])
                if ename == "sp":
                    h.wait_ge(sems["sp"], csum["sp"])
            return run
        blk.tensor(make_runner("pe"))
        blk.vector(make_runner("dve"))
        blk.scalar(make_runner("act"))
        blk.gpsimd(make_runner("pool"))
        blk.sync(make_runner("sp"))

    return nc


_NC_CACHE = None


def _get_nc():
    global _NC_CACHE
    if _NC_CACHE is None:
        _NC_CACHE = build_nc()
    return _NC_CACHE


def kernel(L0_attn_x, L0_attn_Wq, L0_attn_Wk, L0_attn_Wv, L0_attn_Wg1, L0_attn_Wg2,
           L0_attn_bg2, L0_attn_Wg, L0_attn_g_norm_w, L0_attn_Wo, L0_ffn_weight):
    x = np.ascontiguousarray(np.asarray(L0_attn_x, np.float32))
    base = dict(
        wq=np.ascontiguousarray(np.asarray(L0_attn_Wq, np.float32)),
        wk=np.ascontiguousarray(np.asarray(L0_attn_Wk, np.float32)),
        wv=np.ascontiguousarray(np.asarray(L0_attn_Wv, np.float32)),
        wg=np.ascontiguousarray(np.asarray(L0_attn_Wg, np.float32)),
        wo=np.ascontiguousarray(np.asarray(L0_attn_Wo, np.float32)),
        wg1=np.ascontiguousarray(np.asarray(L0_attn_Wg1, np.float32)),
        wg2b=np.ascontiguousarray(np.concatenate(
            [np.asarray(L0_attn_Wg2, np.float32),
             np.asarray(L0_attn_bg2, np.float32)[None, :]], axis=0)),
        wffnT=np.ascontiguousarray(np.asarray(L0_ffn_weight, np.float32).T),
        gnw=np.ascontiguousarray(np.broadcast_to(
            np.tile(np.asarray(L0_attn_g_norm_w, np.float32), NH), (128, DM))),
        ui16=np.triu(np.full((128, 128), 1.0 / 16.0, np.float32)).astype(BF),
        sl16=np.tril(np.full((128, 128), 1.0 / 16.0, np.float32), k=-1).astype(BF),
        masku=np.tile(np.triu(np.ones((128, 128), np.float32)), (1, 4)).astype(BF),
        ident=np.eye(128, dtype=np.float32).astype(BF),
        ones16=np.full((128, 1), 1.0 / 16.0, np.float32).astype(BF),
        onesrow=np.ones((1, 128), np.float32),
        cpool=np.stack([np.full(128, v, np.float32) for v in
                        (ONE, EPS5, EPS8, LN8INV, LN127, NLN127, 0.0, 0.0)], axis=1),
    )
    in_maps = []
    for b in range(8):
        m = dict(base)
        m["x"] = np.ascontiguousarray(x[b])
        m["xT"] = np.ascontiguousarray(x[b].T)
        in_maps.append(m)
    nc = _get_nc()
    res = run_bass_kernel_spmd(nc, in_maps, core_ids=list(range(8)))
    return np.stack([r["out"] for r in res.results], axis=0)


# version 23
# speedup vs baseline: 1.0317x; 1.0289x over previous
import math
import numpy as np
import ml_dtypes
import concourse.bass as bass
import concourse.mybir as mybir
from concourse.bass_utils import run_bass_kernel_spmd

F32 = mybir.dt.float32
BF16 = mybir.dt.bfloat16
I32 = mybir.dt.int32
AF = mybir.ActivationFunctionType
ALU = mybir.AluOpType
AX = mybir.AxisListType

L = 8192
DM = 1024
C = 128
NCH = L // C
NH = 16
HD = 64

LN8INV = float(np.log(np.float32(0.125)))
LN127 = float(np.log(np.float32(127.0)))
NLN127 = -LN127
ONE = 1.0
EPS5 = 1e-5
EPS8 = 1e-8
WMEANSCALE = float(1.0 / (1024.0 * 1024.0))

BF = ml_dtypes.bfloat16

ENGS = ("pe", "dve", "act", "pool", "sp")
PSUM_BANKS = frozenset(["A0", "A1", "B0", "B1", "C0", "C1", "D0", "D1"])
TINY = frozenset(
    ["c3", "c4", "c5", "rs16", "rstd", "ebt", "wst", "brow", "wbc"]
    + ["wacc%d" % j for j in range(8)]
    + ["l1c0_" + n for n in ("negm", "s2c", "rstd")]
    + ["l2c0_" + n for n in ("negm", "s2c", "rstd")]
)
FRESH_BIG = frozenset(["xbf", "gkh", "qtt", "kht", "kbt", "og0", "og1",
                       "aqbf", "scr2", "S_sb", "S_tmp", "S_bd", "gkf", "sgw",
                       "vbf", "qtf", "khf", "ogf", "aqf", "r1f", "x1f", "aqi"])


class Sched:

    def __init__(self):
        self.n = 0
        self.eng = []
        self.fn = []
        self.inc = []
        self.dur = []
        self.deps = []
        self.res = {}
        self.order = None

    def op(self, eng, fn, reads=(), writes=(), inc=1, dur=1000, tiny_safe=False, guard=False):
        oid = self.n
        self.n += 1
        is_dma = inc == 16
        deps = set()
        for r in reads:
            rec = self.res.get(r)
            if rec and rec["w"] is not None:
                deps.add(rec["w"])
            if rec and r in PSUM_BANKS and eng in ("dve", "act"):
                other = "act" if eng == "dve" else "dve"
                if other in rec["r"]:
                    deps.add(rec["r"][other])
        for r in writes:
            rec = self.res.get(r)
            if rec:
                if rec["w"] is not None:
                    deps.add(rec["w"])
                deps.update(rec["r"].values())
        tdeps = []
        fdeps = []
        if not tiny_safe:
            for r in reads:
                rec = self.res.get(r)
                if rec and rec["w"] is not None:
                    if r in TINY:
                        tdeps.append(rec["w"])
                    elif r in FRESH_BIG:
                        fdeps.append(rec["w"])
        delay = 0
        for r in reads:
            rec = self.res.setdefault(r, {"w": None, "r": {}})
            rec["r"][eng] = oid
        for r in writes:
            self.res[r] = {"w": oid, "r": {}}
        self.eng.append(eng)
        self.fn.append(fn)
        self.inc.append(inc)
        self.dur.append(dur)
        self.deps.append(deps)
        if not hasattr(self, "tinydeps"):
            self.tinydeps = []
            self.freshdeps = []
        self.tinydeps.append(tdeps)
        self.freshdeps.append(fdeps)

    def schedule(self):
        import heapq
        n = self.n
        nsucc = [[] for _ in range(n)]
        indeg = [0] * n
        for i in range(n):
            for d in self.deps[i]:
                nsucc[d].append(i)
                indeg[i] += 1
        cp = [0.0] * n
        for i in range(n - 1, -1, -1):
            m = max((cp[j] for j in nsucc[i]), default=0.0)
            cp[i] = self.dur[i] + m
        self.cp = cp
        fin = [0.0] * n
        start = [0.0] * n
        t_eng = {e: 0.0 for e in ENGS}
        ready = {e: [] for e in ENGS}
        sched_fin = [None] * n
        for i in range(n):
            if indeg[i] == 0:
                heapq.heappush(ready[self.eng[i]], i)
        out = {e: [] for e in ENGS}
        self.dispatch = []
        done = 0
        while done < n:
            best = None
            for e in ENGS:
                if not ready[e]:
                    continue
                cands = heapq.nsmallest(40, ready[e])
                for i in cands:
                    st = max(t_eng[e], max((fin[d] for d in self.deps[i]), default=0.0))
                    key = (st, i)
                    if best is None or key < best[0]:
                        best = (key, e, i)
            _, e, i = best
            ready[e].remove(i)
            heapq.heapify(ready[e])
            st = max(t_eng[e], max((fin[d] for d in self.deps[i]), default=0.0))
            start[i] = st
            fin[i] = st + self.dur[i]
            t_eng[e] = fin[i]
            out[e].append(i)
            self.dispatch.append(i)
            done += 1
            for j in nsucc[i]:
                indeg[j] -= 1
                if indeg[j] == 0:
                    heapq.heappush(ready[self.eng[j]], j)
        self.order = out
        self.start = start
        self.finv = fin
        self.makespan = max(t_eng.values())
        return out

def build_nc(nch=NCH, do_init_ffn=True, do_loop=True, op_limit=None, dump=False):
    nc = bass.Bass("TRN2", target_bir_lowering=False)

    x_d = nc.dram_tensor("x", [L, DM], F32, kind="ExternalInput").ap()
    xT_d = nc.dram_tensor("xT", [DM, L], F32, kind="ExternalInput").ap()
    wq_d = nc.dram_tensor("wq", [DM, DM], F32, kind="ExternalInput").ap()
    wk_d = nc.dram_tensor("wk", [DM, DM], F32, kind="ExternalInput").ap()
    wv_d = nc.dram_tensor("wv", [DM, DM], F32, kind="ExternalInput").ap()
    wg_d = nc.dram_tensor("wg", [DM, DM], F32, kind="ExternalInput").ap()
    wo_d = nc.dram_tensor("wo", [DM, DM], F32, kind="ExternalInput").ap()
    wg1_d = nc.dram_tensor("wg1", [DM, 16], F32, kind="ExternalInput").ap()
    wg2b_d = nc.dram_tensor("wg2b", [17, DM], F32, kind="ExternalInput").ap()
    wffnT_d = nc.dram_tensor("wffnT", [DM, DM], F32, kind="ExternalInput").ap()
    gnw_d = nc.dram_tensor("gnw", [128, DM], F32, kind="ExternalInput").ap()
    ui16_d = nc.dram_tensor("ui16", [128, 128], BF16, kind="ExternalInput").ap()
    sl16_d = nc.dram_tensor("sl16", [128, 128], BF16, kind="ExternalInput").ap()
    masku_d = nc.dram_tensor("masku", [128, 512], BF16, kind="ExternalInput").ap()
    ident_d = nc.dram_tensor("ident", [128, 128], BF16, kind="ExternalInput").ap()
    ones16_d = nc.dram_tensor("ones16", [128, 1], BF16, kind="ExternalInput").ap()
    onesrow_d = nc.dram_tensor("onesrow", [1, 128], F32, kind="ExternalInput").ap()
    cpool_d = nc.dram_tensor("cpool", [128, 8], F32, kind="ExternalInput").ap()
    ws_scr = nc.dram_tensor("ws_scr", [128, 8], F32).ap()
    bc_scr = nc.dram_tensor("bc_scr", [1, 4], F32).ap()
    out_d = nc.dram_tensor("out", [L, DM], F32, kind="ExternalOutput").ap()

    A = nc.alloc_sbuf_tensor
    w_q = A("w_q", [128, 8192], BF16)
    w_k = A("w_k", [128, 8192], BF16)
    w_v = A("w_v", [128, 8192], BF16)
    w_g = A("w_g", [128, 8192], BF16)
    w_o = A("w_o", [128, 8192], BF16)
    w_fT = A("w_fT", [128, 8192], BF16)
    w_g1 = A("w_g1", [128, 128], BF16)
    w_g2b = A("w_g2b", [17, 1024], BF16)
    ui16 = A("sb_ui16", [128, 128], BF16)
    sl16 = A("sb_sl16", [128, 128], BF16)
    masku = A("sb_masku", [128, 512], BF16)
    ident = A("sb_ident", [128, 128], BF16)
    ones16 = A("sb_ones16", [128, 1], BF16)
    onesrow = A("sb_onesrow", [1, 128], F32)
    onescol = A("onescol", [128, 1], F32)
    cpool = A("sb_cpool", [128, 8], F32)
    gnw = A("sb_gnw", [128, DM], F32)

    xtok = [A("xtok0", [128, DM], F32), A("xtok1", [128, DM], F32)]
    outb = [A("outb0", [128, DM], F32), A("outb1", [128, DM], F32)]
    xfeat = [A("xfeat0", [128, DM], BF16), A("xfeat1", [128, DM], BF16)]
    xbf = A("xbf", [128, DM], BF16)
    t1b = A("t1b", [32, 128], BF16)

    scr1 = A("scr1", [128, DM], F32)
    scr2 = A("scr2", [128, DM], F32)
    gkf = A("gkf", [128, DM], F32)
    expB = A("expB", [128, DM], F32)
    expnB = A("expnB", [128, DM], F32)
    expBrev = A("expBrev", [128, DM], F32)
    sq = A("sq", [128, DM], F32)
    r1f = A("r1f", [128, DM], F32)
    x1f = A("x1f", [128, DM], F32)

    gkh = A("gkh", [128, DM], BF16)
    qtt = A("qtt", [128, DM], BF16)
    kht = A("kht", [128, DM], BF16)
    kbt = A("kbt", [128, DM], BF16)
    qtf = A("qtf", [128, DM], BF16)
    khf = A("khf", [128, DM], BF16)
    vbf = A("vbf", [128, DM], BF16)
    og = [A("og0", [128, DM], BF16), A("og1", [128, DM], BF16)]
    ogf = A("ogf", [128, DM], BF16)
    aqbf = A("aqbf", [128, DM], BF16)
    aqf = A("aqf", [128, DM], BF16)
    aqi = A("aqi", [128, DM], I32)
    pm = [A("pm0", [128, 512], BF16), A("pm1", [128, 512], BF16)]

    S_sb = A("S_sb", [128, 512], F32)
    S_tmp = A("S_tmp", [128, 512], F32)
    S_bd = A("S_bd", [128, DM], BF16)
    ebt = A("ebt", [128, 8], F32)

    rs16 = A("rs16", [128, 16], F32)
    rstd = A("rstd", [128, 16], F32)
    cst1 = A("cst1", [128, 16], F32)
    cst2 = A("cst2", [128, 16], F32)
    c3 = A("c3", [128, 1], F32)
    c4 = A("c4", [128, 1], F32)
    c5 = A("c5", [128, 1], F32)

    wst = A("wst", [1, 1], F32)
    wrow = A("wrow", [1, 1024], F32)
    brow = A("brow", [1, 4], F32)
    wbc = A("wbc", [128, 4], F32)
    wacc = A("wacc", [128, 8], F32)
    wsum = A("wsum", [128, 1], F32)

    nc.const_aps.register(F32, ONE, cpool[:, 0:1])
    nc.const_aps.register(F32, EPS5, cpool[:, 1:2])
    nc.const_aps.register(F32, EPS8, cpool[:, 2:3])
    nc.const_aps.register(F32, LN8INV, cpool[:, 3:4])
    nc.const_aps.register(F32, LN127, cpool[:, 4:5])
    nc.const_aps.register(F32, NLN127, cpool[:, 5:6])

    pA = nc.alloc_psum_tensor("pA", [128, 1024], F32)
    pB = nc.alloc_psum_tensor("pB", [128, 1024], F32)
    pC = nc.alloc_psum_tensor("pC", [128, 1024], F32)
    pD = nc.alloc_psum_tensor("pD", [128, 1024], F32)
    pB16 = pB[:].bitcast(BF16)

    sems = {e: nc.alloc_semaphore("sem_" + e) for e in ENGS}
    sch = Sched()

    def proj_mm(psum, lhs_src, w_sb, tag):
        if not callable(lhs_src):
            ap = lhs_src
            lhs_src = lambda k: ap[:, 128 * k:128 * (k + 1)]
        def f(h):
            last = None
            for half in range(2):
                o = psum[:, 512 * half:512 * (half + 1)]
                for k in range(8):
                    last = h.matmul(out=o, lhsT=lhs_src(k),
                                    rhs=w_sb[:, 1024 * k + 512 * half:1024 * k + 512 * half + 512],
                                    start=(k == 0), stop=(k == 7))
            return last
        return f

    def transpose8(psum16, coloff, src, n=8):
        def f(h):
            last = None
            for j in range(n):
                last = h.transpose(out=psum16[:, coloff + 128 * j:coloff + 128 * (j + 1)],
                                   in_=src[:, 128 * j:128 * (j + 1)], identity=ident[:])
            return last
        return f

    def v3(ap, g=16):
        return ap.rearrange("p (g f) -> p g f", g=g)

    def bcast(ap2d, g, f):
        return ap2d.rearrange("p (g o) -> p g o", o=1).broadcast_to([128, g, f])

    def emit_ln(src, srcname, junk, junkname, sqb, sqname, cst, outap, outname, pref):
        negm, s2c, rstd = cst[:, 1:2], cst[:, 2:3], cst[:, 3:4]
        P = pref
        sch.op("act", lambda h: h.activation(out=junk, in_=src, func=AF.Identity,
                                             scale=-1.0 / 1024.0, accum_out=negm),
               reads=[srcname], writes=[junkname, P + "negm"], dur=700)
        sch.op("act", lambda h: h.activation(out=sqb, in_=src, func=AF.Square, bias=negm,
                                             accum_out=s2c),
               reads=[srcname, P + "negm"], writes=[sqname, P + "s2c"], dur=700)
        sch.op("act", lambda h: h.activation(out=rstd, in_=s2c, func=AF.Ln,
                                             scale=1.0 / 1024.0, bias=EPS5),
               reads=[P + "s2c", "cst"], writes=[P + "rstd"], dur=250)
        sch.op("act", lambda h: h.activation(out=rstd, in_=rstd, func=AF.Exp, scale=-0.5),
               reads=[P + "rstd"], writes=[P + "rstd"], dur=250, tiny_safe=True)
        sch.op("dve", lambda h: h.tensor_scalar(out=outap, in0=src, scalar1=negm,
                                                scalar2=rstd, op0=ALU.add, op1=ALU.mult),
               reads=[srcname, P + "negm", P + "rstd"], writes=[outname], dur=700)

    for dst, src in [(ui16, ui16_d), (sl16, sl16_d), (masku, masku_d),
                     (ident, ident_d), (ones16, ones16_d),
                     (onesrow[0:1, :], onesrow_d), (cpool, cpool_d), (gnw, gnw_d)]:
        sch.op("sp", (lambda d, s: (lambda h: h.dma_start(out=d, in_=s)))(
            dst[:] if hasattr(dst, "ap") else dst, src),
            writes=["cst"], inc=16)
    for name, dst, src, a in [("w_q", w_q, wq_d, 8), ("w_k", w_k, wk_d, 8),
                              ("w_v", w_v, wv_d, 8), ("w_g", w_g, wg_d, 8),
                              ("w_o", w_o, wo_d, 8)]:
        dview = dst[:].rearrange("p (a n) -> p a n", a=a)
        sview = src.rearrange("(a p) n -> p a n", p=128)
        sch.op("pool", (lambda d, s: (lambda h: h.dma_start(out=d, in_=s)))(dview, sview),
               writes=[name], inc=16)
    sch.op("pool", lambda h: h.dma_start(
        out=w_g1[:].rearrange("p (a n) -> p a n", n=16),
        in_=wg1_d.rearrange("(a p) n -> p a n", p=128)), writes=["w_g1"], inc=16)
    sch.op("pool", lambda h: h.dma_start(out=w_g2b[0:17, :], in_=wg2b_d),
           writes=["w_g2b"], inc=16)

    sch.op("dve", lambda h: h.memset(onescol[:], 1.0), writes=["onescol"])
    sch.op("dve", lambda h: h.memset(t1b[:], 1.0), writes=["t1b_ones"])
    sch.op("dve", lambda h: h.memset(S_sb[:], 0.0), writes=["S_sb"])
    sch.op("dve", lambda h: h.memset(S_bd[:], 0.0), writes=["S_bd"])

    if not do_init_ffn:
        wfT_all = []

    for j in range(8 if do_init_ffn else 0):
        buf, rname = (scr1, "scr1") if j % 2 == 0 else (scr2, "scr2")
        sch.op("sp", (lambda b, jj: (lambda h: h.dma_start(out=b[:], in_=wffnT_d[128 * jj:128 * (jj + 1), :])))(buf, j),
               writes=[rname], inc=16)
        sch.op("dve", (lambda b, jj: (lambda h: h.reduce_sum(
            out=wacc[:, jj:jj + 1], in_=b[:], axis=AX.X, apply_absolute_value=True)))(buf, j),
            reads=[rname], writes=["wacc%d" % j])
    if do_init_ffn:
      sch.op("act", lambda h: h.copy(out=ebt[:], in_=wacc[:]),
           reads=["wacc%d" % j for j in range(8)], writes=["ebt"])
    if do_init_ffn:
      def f_wsscr(h):
          h.nop(cycle_cnt=4000)
          return h.dma_start(out=ws_scr, in_=ebt[:])
      sch.op("sp", f_wsscr, reads=["ebt"], writes=["ws_scr"], inc=16)
      sch.op("sp", lambda h: h.dma_start(
          out=wrow[0:1, :], in_=ws_scr.rearrange("(o a) b -> o (a b)", o=1)),
          reads=["ws_scr"], writes=["wrow"], inc=16)
      sch.op("act", lambda h: h.activation(out=wrow[0:1, :], in_=wrow[0:1, :],
                                           func=AF.Identity, accum_out=wst[0:1, :]),
             reads=["wrow"], writes=["wrow", "wst"])
      sch.op("act", lambda h: h.activation(out=wst[0:1, :], in_=wst[0:1, :], func=AF.Ln,
                                           scale=WMEANSCALE, bias=EPS8),
             reads=["wst", "cst"], writes=["wst"])
      sch.op("act", lambda h: h.activation(out=brow[0:1, 0:1], in_=wst[0:1, :],
                                           func=AF.Exp, scale=-1.0),
             reads=["wst"], writes=["brow"])
      sch.op("act", lambda h: h.activation(out=brow[0:1, 1:2], in_=wst[0:1, :],
                                           func=AF.Exp, bias=NLN127),
             reads=["wst", "cst"], writes=["brow"])
      sch.op("act", lambda h: h.activation(out=brow[0:1, 2:3], in_=brow[0:1, 1:2],
                                           func=AF.Identity, scale=EPS8),
             reads=["brow"], writes=["brow"])
      def f_bcscr(h):
          h.nop(cycle_cnt=4000)
          return h.dma_start(out=bc_scr, in_=brow[0:1, :])
      sch.op("sp", f_bcscr, reads=["brow"], writes=["bc_scr"], inc=16)
      sch.op("sp", lambda h: h.dma_start(out=wbc[:], in_=bc_scr.broadcast_to([128, 4])),
             reads=["bc_scr"], writes=["wbc"], inc=16)
    for j in ([0, 1, 2, 3, 4, 5, 6, 7, 0] if do_init_ffn else []):
        sch.op("sp", (lambda jj: (lambda h: h.dma_start(out=scr1[:], in_=wffnT_d[128 * jj:128 * (jj + 1), :])))(j),
               writes=["scr1"], inc=16)
        sch.op("dve", lambda h: h.tensor_scalar(out=aqi[:], in0=scr1[:], scalar1=wbc[:, 0:1],
                                                scalar2=None, op0=ALU.mult),
               reads=["scr1", "wbc"], writes=["aqi"])
        def f_clip(h):
            h.nop(cycle_cnt=4000)
            return h.tensor_scalar(out=scr2[:], in0=aqi[:], scalar1=1.0,
                                   scalar2=-1.0, op0=ALU.min, op1=ALU.max)
        sch.op("dve", f_clip, reads=["aqi"], writes=["scr2"])
        def f_wft(h, jj=j):
            h.nop(cycle_cnt=4000)
            return h.copy(out=w_fT[:, 1024 * jj:1024 * (jj + 1)], in_=scr2[:])
        sch.op("act", f_wft, reads=["scr2"], writes=["w_fT%d" % j])

    if do_init_ffn:
        wfT_all = ["w_fT%d" % j for j in range(8)]

    init_gate = (["w_fT%d" % j for j in range(8)] + ["wbc", "onescol", "t1b_ones",
                 "S_sb", "S_bd", "cst"] + list(wfT_all)) if do_init_ffn else ["cst"]
    for e in ("pe", "dve", "act", "pool"):
        sch.op(e, (lambda: (lambda h: h.nop(cycle_cnt=8000)))(),
               reads=init_gate, writes=["bar_" + e], dur=7000)
    init_gate = init_gate + ["bar_pe", "bar_dve", "bar_act", "bar_pool"]
    for c0 in range(min(2, nch if do_loop else 0)):
        def f_prime(h, cc=c0):
            h.nop(cycle_cnt=4000)
            return h.dma_start(out=xtok[cc % 2][:], in_=x_d[128 * cc:128 * (cc + 1), :])
        sch.op("sp", f_prime, reads=init_gate,
               writes=["xtok%d" % (c0 % 2)], inc=16)

    sch.limit = op_limit
    pC16 = pC[:].bitcast(BF16)
    pD16 = pD[:].bitcast(BF16)

    def emit_head_mid(c):
        guard = c < 3
        _schop = sch.op
        def sopg(*a, **kw):
            kw.setdefault("guard", guard)
            return _schop(*a, **kw)
        sch_op_saved = sch.op
        p = c % 2
        XT = "xtok%d" % p
        XF = "xfeat%d" % p
        OG = "og%d" % p
        xfb = xfeat[p]

        def xf_slice(k):
            return xfb[:, 128 * k:128 * (k + 1)]

        sopg("pool", (lambda pp: (lambda h: h.tensor_copy(out=xbf[:], in_=xtok[pp][:])))(p),
               reads=[XT], writes=["xbf"], dur=1200)
        sopg("pe", transpose8(pC16, 0, xbf[:]), reads=["xbf", "cst"], writes=["C0"], dur=2600)
        sopg("act", (lambda pp: (lambda h: h.copy(out=xfeat[pp][:], in_=pC16[:, 0:1024])))(p),
               reads=["C0"], writes=[XF], dur=420)

        def f_t1(h):
            last = None
            for k in range(8):
                last = h.matmul(out=pC[0:16, 0:128], lhsT=w_g1[:, 16 * k:16 * (k + 1)],
                                rhs=xf_slice(k),
                                start=(k == 0), stop=(k == 7))
            return last
        sopg("pe", f_t1, reads=[XF, "w_g1"], writes=["C0"], dur=800)
        sopg("act", lambda h: h.copy(out=t1b[0:16, :], in_=pC[0:16, 0:128]),
               reads=["C0"], writes=["t1b"], dur=300)

        def f_z(h):
            last = None
            for half in range(2):
                last = h.matmul(out=pC[:, 512 * half:512 * (half + 1)], lhsT=t1b[0:17, :],
                                rhs=w_g2b[0:17, 512 * half:512 * (half + 1)],
                                start=True, stop=True)
            return last
        sopg("pe", f_z, reads=["t1b", "t1b_ones", "w_g2b"], writes=["C0", "C1"], dur=500)

        sopg("act", lambda h: h.activation(out=scr1[:], in_=pC[:], func=AF.Abs),
               reads=["C0", "C1"], writes=["scr1"], dur=620)
        sopg("act", lambda h: h.activation(out=scr2[:], in_=scr1[:], func=AF.Exp, scale=-1.0),
               reads=["scr1"], writes=["scr2"], dur=620)
        sopg("act", lambda h: h.activation(out=scr1[:], in_=scr2[:], func=AF.Ln, bias=ONE),
               reads=["scr2", "cst"], writes=["scr1"], dur=620)
        sopg("dve", lambda h: h.tensor_scalar(out=scr2[:], in0=pC[:], scalar1=0.0,
                                                scalar2=None, op0=ALU.min),
               reads=["C0", "C1"], writes=["scr2"], dur=660)
        sopg("pool", lambda h: h.tensor_tensor(out=gkh[:], in0=scr2[:], in1=scr1[:], op=ALU.subtract),
               reads=["scr1", "scr2"], writes=["gkh"], dur=2500)

        def f_btot(h):
            last = None
            for j in range(8):
                last = h.matmul(out=pD[:, j:j + 1], lhsT=gkh[:, 128 * j:128 * (j + 1)],
                                rhs=ones16[:], start=True, stop=True)
            return last
        sopg("pe", f_btot, reads=["gkh", "cst"], writes=["D0"], dur=700)
        sopg("act", lambda h: h.activation(out=ebt[:], in_=pD[:, 0:8], func=AF.Exp),
               reads=["D0"], writes=["ebt"], dur=250)

        def f_B(h):
            last = None
            for half in range(2):
                last = h.matmul(out=pD[:, 512 * half:512 * (half + 1)], lhsT=ui16[:],
                                rhs=gkh[:, 512 * half:512 * half + 512], start=True, stop=True)
            return last
        sopg("pe", f_B, reads=["gkh", "cst"], writes=["D0", "D1"], dur=550)

        def f_Brev(h):
            last = None
            for half in range(2):
                last = h.matmul(out=pC[:, 512 * half:512 * (half + 1)], lhsT=sl16[:],
                                rhs=gkh[:, 512 * half:512 * half + 512], start=True, stop=True)
            return last
        sopg("pe", f_Brev, reads=["gkh", "cst"], writes=["C0", "C1"], dur=550)

        sopg("act", lambda h: h.activation(out=expB[:], in_=pD[:], func=AF.Exp, bias=LN8INV),
               reads=["D0", "D1", "cst"], writes=["expB"], dur=620)
        sopg("act", lambda h: h.activation(out=expnB[:], in_=pD[:], func=AF.Exp, scale=-1.0),
               reads=["D0", "D1"], writes=["expnB"], dur=620)
        sopg("act", lambda h: h.activation(out=expBrev[:], in_=pC[:], func=AF.Exp),
               reads=["C0", "C1"], writes=["expBrev"], dur=620)

        sopg("pe", proj_mm(pC, xf_slice, w_g, "g"), reads=[XF, "w_g"], writes=["C0", "C1"], dur=3900)
        sopg("act", lambda h: h.activation(out=scr1[:], in_=pC[:], func=AF.Abs),
               reads=["C0", "C1"], writes=["scr1"], dur=620)
        sopg("act", lambda h: h.activation(out=scr1[:], in_=scr1[:], func=AF.Exp, scale=-1.0),
               reads=["scr1"], writes=["scr1"], dur=620)
        sopg("act", lambda h: h.activation(out=scr1[:], in_=scr1[:], func=AF.Ln, bias=ONE),
               reads=["scr1", "cst"], writes=["scr1"], dur=620)
        sopg("dve", lambda h: h.tensor_scalar(out=scr2[:], in0=pC[:], scalar1=0.0,
                                                scalar2=None, op0=ALU.min),
               reads=["C0", "C1"], writes=["scr2"], dur=660)
        sopg("dve", lambda h: h.tensor_tensor(out=scr2[:], in0=scr2[:], in1=scr1[:], op=ALU.subtract),
               reads=["scr1", "scr2"], writes=["scr2"], dur=1130)
        sopg("act", lambda h: h.activation(out=gkf[:], in_=scr2[:], func=AF.Exp),
               reads=["scr2"], writes=["gkf"], dur=620)
        sopg("dve", lambda h: h.scalar_tensor_tensor(out=scr1[:], in0=pC[:], scalar=1.0,
                                                       in1=gkf[:], op0=ALU.mult, op1=ALU.mult),
               reads=["C0", "C1", "gkf"], writes=["scr1"], dur=1200)
        sopg("pool", lambda h: h.tensor_tensor(out=scr2[:], in0=scr1[:], in1=gnw[:], op=ALU.mult),
               reads=["scr1", "cst"], writes=["scr2"], dur=2800)

        sopg("pe", proj_mm(pD, xf_slice, w_q, "q"), reads=[XF, "w_q"], writes=["D0", "D1"], dur=3900)
        sopg("dve", lambda h: h.tensor_tensor(out=qtt[:], in0=pD[:], in1=expB[:], op=ALU.mult),
               reads=["D0", "D1", "expB"], writes=["qtt"], dur=1200)
        sopg("pe", proj_mm(pD, xf_slice, w_k, "k"), reads=[XF, "w_k"], writes=["D0", "D1"], dur=3900)
        sopg("dve", lambda h: h.tensor_tensor(out=kht[:], in0=pD[:], in1=expnB[:], op=ALU.mult),
               reads=["D0", "D1", "expnB"], writes=["kht"], dur=1200)
        sopg("dve", lambda h: h.tensor_tensor(out=kbt[:], in0=pD[:], in1=expBrev[:], op=ALU.mult),
               reads=["D0", "D1", "expBrev"], writes=["kbt"], dur=1200)

        sopg("pe", transpose8(pD16, 0, qtt[:]), reads=["qtt", "cst"], writes=["D0"], dur=2600)
        sopg("pe", transpose8(pD16, 1024, kht[:]), reads=["kht", "cst"], writes=["D1"], dur=2600)
        sopg("dve", lambda h: h.tensor_copy(out=qtf[:], in_=pD16[:, 0:1024]), reads=["D0"], writes=["qtf"], dur=420)
        sopg("dve", lambda h: h.tensor_copy(out=khf[:], in_=pD16[:, 1024:2048]), reads=["D1"], writes=["khf"], dur=420)

        sopg("pe", proj_mm(pD, xf_slice, w_v, "v"), reads=[XF, "w_v"], writes=["D0", "D1"], dur=3900)
        sopg("act", lambda h: h.copy(out=vbf[:], in_=pD[:]), reads=["D0", "D1"], writes=["vbf"], dur=620)

        for b8 in range(2):
            def f_P8(h, bb=b8):
                last = None
                for i in range(8):
                    hh = 8 * bb + i
                    b = 64 * (hh % 2)
                    j = hh // 2
                    last = h.matmul(out=pC[:, 512 * (hh % 2) + 128 * (j % 4):
                                            512 * (hh % 2) + 128 * (j % 4) + 128],
                                    lhsT=khf[b:b + 64, 128 * j:128 * (j + 1)],
                                    rhs=qtf[b:b + 64, 128 * j:128 * (j + 1)],
                                    start=True, stop=True)
                return last
            sopg("pe", f_P8, reads=["khf", "qtf"], writes=["C0", "C1"], dur=1100)
            sopg("dve", lambda h: h.tensor_tensor(
                out=pm[0][:], in0=pC[:, 0:512], in1=masku[:], op=ALU.mult),
                reads=["C0", "cst"], writes=["pm0"], dur=660)
            sopg("dve", lambda h: h.tensor_tensor(
                out=pm[1][:], in0=pC[:, 512:1024], in1=masku[:], op=ALU.mult),
                reads=["C1", "cst"], writes=["pm1"], dur=660)

            for jp in range(4 * b8, 4 * b8 + 4):
                def f_O(h, j=jp):
                    last = h.matmul(out=pD[:, 128 * j:128 * (j + 1)],
                                    lhsT=qtf[:, 128 * j:128 * (j + 1)],
                                    rhs=S_bd[:, 128 * j:128 * (j + 1)],
                                    start=True, stop=False)
                    for t in range(2):
                        hh = 2 * j + t
                        last = h.matmul(out=pD[:, 64 * hh:64 * (hh + 1)],
                                        lhsT=pm[t][:, 128 * (j % 4):128 * (j % 4) + 128],
                                        rhs=vbf[:, 64 * hh:64 * (hh + 1)],
                                        start=False, stop=(t == 1))
                    return last
                sopg("pe", f_O, reads=["qtf", "S_bd", "pm0", "pm1", "vbf"],
                       writes=["D%d" % (jp // 4)], dur=400)

        def f_S(h, h0, h1):
            last = None
            for hh in range(h0, h1):
                b = 64 * (hh % 2)
                j = hh // 2
                last = h.matmul(out=pC[b:b + 64, 64 * j:64 * (j + 1)],
                                lhsT=kbt[:, 64 * hh:64 * (hh + 1)],
                                rhs=vbf[:, 64 * hh:64 * (hh + 1)],
                                start=True, stop=True)
            return last
        sopg("pe", (lambda fS: (lambda h: fS(h, 0, 8)))(f_S), reads=["kbt", "vbf"], writes=["C0"], dur=900)
        sopg("pe", (lambda fS: (lambda h: fS(h, 8, 16)))(f_S), reads=["kbt", "vbf"], writes=["C0"], dur=900)

        sopg("dve", lambda h: h.tensor_tensor(out=v3(S_tmp[:], 8), in0=v3(S_sb[:], 8),
                                                in1=bcast(ebt[:], 8, 64), op=ALU.mult),
               reads=["S_sb", "ebt"], writes=["S_tmp"], dur=620)
        sopg("dve", lambda h: h.tensor_tensor(out=S_sb[:], in0=pC[:, 0:512], in1=S_tmp[:], op=ALU.add),
               reads=["C0", "S_tmp"], writes=["S_sb"], dur=660)
        sopg("act", lambda h: h.copy(
            out=S_bd[0:64, :].rearrange("p (j n) -> p j n", j=8)[:, :, 0:64],
            in_=v3(S_sb[0:64, :], 8)), reads=["S_sb"], writes=["S_bd"], dur=450)
        sopg("act", lambda h: h.copy(
            out=S_bd[64:128, :].rearrange("p (j n) -> p j n", j=8)[:, :, 64:128],
            in_=v3(S_sb[64:128, :], 8)), reads=["S_sb"], writes=["S_bd"], dur=450)

        sopg("act", lambda h: h.activation(out=sq[:], in_=pD[:], func=AF.Square),
               reads=["D0", "D1"], writes=["sq"], dur=620)
        sopg("dve", lambda h: h.reduce_sum(out=rs16[:], in_=v3(sq[:], 16), axis=AX.X),
               reads=["sq"], writes=["rs16"], dur=1130)
        sopg("act", lambda h: h.activation(out=rs16[:], in_=rs16[:], func=AF.Ln,
                                             scale=1.0 / 64.0, bias=EPS5),
               reads=["rs16", "cst"], writes=["rs16"], dur=250)
        sopg("act", lambda h: h.activation(out=rstd[:], in_=rs16[:], func=AF.Exp, scale=-0.5),
               reads=["rs16"], writes=["rstd"], dur=250, tiny_safe=True)
        sopg("dve", lambda h: h.tensor_tensor(out=v3(gkf[:], 16), in0=v3(pD[:], 16),
                                                in1=bcast(rstd[:], 16, 64), op=ALU.mult),
               reads=["D0", "D1", "rstd"], writes=["gkf"], dur=1200)
        sopg("dve", (lambda pp: (lambda h: h.tensor_tensor(
            out=og[pp][:], in0=gkf[:], in1=scr2[:], op=ALU.mult)))(p),
            reads=["gkf", "scr2"], writes=[OG], dur=1130)

    def emit_tail(c):
        guard = c < 3
        _schop = sch.op
        def sopg(*a, **kw):
            kw.setdefault("guard", guard)
            return _schop(*a, **kw)
        p = c % 2
        XT = "xtok%d" % p
        OG = "og%d" % p

        sopg("pe", transpose8(pB16, 0, og[p][:]), reads=[OG, "cst"], writes=["B0"], dur=2600)
        sopg("dve", lambda h: h.tensor_copy(out=ogf[:], in_=pB16[:, 0:1024]), reads=["B0"], writes=["ogf"], dur=420)
        sopg("pe", proj_mm(pA, ogf[:], w_o, "wo"), reads=["ogf", "w_o"], writes=["A0", "A1"], dur=3900)
        sopg("dve", (lambda pp: (lambda h: h.tensor_tensor(
            out=r1f[:], in0=pA[:], in1=xtok[pp][:], op=ALU.add)))(p),
            reads=["A0", "A1", XT], writes=["r1f"], dur=1200)

        if dump and c == 0:
            sopg("pool", lambda h: h.dma_start(out=out_d[4096:4224, :], in_=r1f[:]),
                   reads=["r1f"], writes=["dumpr1"], inc=16)
        if c + 2 < nch:
            sopg("sp", (lambda cc: (lambda h: h.dma_start(
                out=xtok[cc % 2][:], in_=x_d[128 * cc:128 * (cc + 1), :])))(c + 2),
                writes=[XT], inc=16, dur=3500)

        emit_ln(r1f[:], "r1f", sq[:], "sq", sq[:], "sq", cst1, x1f[:], "x1f", "l1c0_")

        sopg("dve", lambda h: h.reduce_max(out=c3[:], in_=x1f[:], axis=AX.X,
                                             apply_absolute_value=True),
               reads=["x1f"], writes=["c3"], dur=1130)
        sopg("act", lambda h: h.activation(out=c4[:], in_=c3[:], func=AF.Ln, bias=EPS8),
               reads=["c3", "cst"], writes=["c4"], dur=250)
        sopg("act", lambda h: h.activation(out=c4[:], in_=c4[:], func=AF.Exp, scale=-1.0, bias=LN127),
               reads=["c4", "cst"], writes=["c4"], dur=250, tiny_safe=True)
        sopg("act", lambda h: h.activation(out=c5[:], in_=c3[:], func=AF.Identity,
                                             scale=wbc[:, 1:2], bias=wbc[:, 2:3]),
               reads=["c3", "wbc"], writes=["c5"], dur=250)
        sopg("dve", lambda h: h.tensor_scalar(out=aqi[:], in0=x1f[:], scalar1=c4[:, 0:1],
                                                scalar2=None, op0=ALU.mult),
               reads=["x1f", "c4"], writes=["aqi"], dur=660)
        sopg("dve", lambda h: h.tensor_copy(out=aqbf[:], in_=aqi[:]), reads=["aqi"], writes=["aqbf"], dur=1130)
        sopg("pe", transpose8(pB16, 1024, aqbf[:]), reads=["aqbf", "cst"], writes=["B1"], dur=2600)
        sopg("dve", lambda h: h.tensor_copy(out=aqf[:], in_=pB16[:, 1024:2048]), reads=["B1"], writes=["aqf"], dur=420)
        sopg("pe", proj_mm(pA, aqf[:], w_fT, "ffn"), reads=["aqf"] + wfT_all, writes=["A0", "A1"], dur=3900)
        sopg("dve", lambda h: h.scalar_tensor_tensor(out=r1f[:], in0=pA[:], scalar=c5[:, 0:1],
                                                       in1=x1f[:], op0=ALU.mult, op1=ALU.add),
               reads=["A0", "A1", "c5", "x1f"], writes=["r1f"], dur=1200)

        emit_ln(r1f[:], "r1f", sq[:], "sq", sq[:], "sq", cst2,
                outb[p][:], "outb%d" % p, "l2c0_")
        def f_store(h, cc=c, pp=p):
            h.nop(cycle_cnt=2500)
            return h.dma_start(out=out_d[128 * cc:128 * (cc + 1), :], in_=outb[pp][:])
        sopg("sp", f_store, reads=["outb%d" % p], inc=16, dur=5500)

    for c in range(nch if do_loop else 0):
        emit_head_mid(c)
        if c >= 1:
            emit_tail(c - 1)
    if do_loop and nch > 0:
        emit_tail(nch - 1)

    if dump:
        sch.limit = None
        dumps = [("gkh", gkh, BF16), ("expB", expB, F32), ("expnB", expnB, F32),
                 ("expBrev", expBrev, F32), ("qtt", qtt, BF16), ("kht", kht, BF16),
                 ("kbt", kbt, BF16), ("vbf", vbf, BF16), ("qtf", qtf, BF16),
                 ("khf", khf, BF16), ("og", og[0], BF16), ("ogf", ogf, BF16),
                 ("r1f", r1f, F32), ("x1f", x1f, F32), ("aqbf", aqbf, BF16),
                 ("S_sb", S_sb, F32), ("ebt", ebt, F32),
                 ("pm0", pm[0], BF16), ("pm1", pm[1], BF16), ("scr2gate", scr2, F32),
                 ("wbc", wbc, F32), ("aqf", aqf, BF16),
                 ("wacc", wacc, F32), ("wsum", wsum, F32), ("cst1", cst1, F32),
                 ("cst2", cst2, F32)]
        for jj in range(8):
            sch.op("pool", (lambda j2: (lambda h: h.dma_start(
                out=out_d[4352 + 128 * j2:4480 + 128 * j2, :],
                in_=w_fT[:, 1024 * j2:1024 * (j2 + 1)])))(jj),
                reads=["w_fT%d" % jj], writes=["dumpwft%d" % jj], inc=16)
        for di, (rname, buf, dt) in enumerate(dumps):
            row = 128 * (8 + di)
            wdt = buf[:].shape[1]
            sch.op("pool", (lambda b, r, w: (lambda h: h.dma_start(
                out=out_d[r:r + 128, 0:w], in_=b[:],
                allow_slow_non_contiguous=True)))(buf, row, wdt),
                reads=[rname], writes=["dump%d" % di], inc=16)
        globals()["DUMP_ROWS"] = {rname: 128 * (8 + di) for di, (rname, _, _) in enumerate(dumps)}

    base_dur = list(sch.dur)
    order = sch.schedule()
    def place_protection():
        sch.delays = [0] * sch.n
        for i in range(sch.n):
            for d in sch.tinydeps[i]:
                if sch.start[i] - sch.finv[d] < 4000.0:
                    sch.delays[i] = 1500
            for d in sch.freshdeps[i]:
                if sch.start[i] - sch.finv[d] < 1800.0:
                    sch.delays[i] = max(sch.delays[i], 1500)
        return {}

    hoists = place_protection()
    for _ in range(2):
        sch.dur = [base_dur[i] + sch.delays[i] * 1.25 for i in range(sch.n)]
        order = sch.schedule()
        hoists = place_protection()
    globals()["LAST_SCHED"] = sch
    count = {}
    csum = {e: 0 for e in ENGS}
    pos = {}
    for e in ENGS:
        for k, i in enumerate(order[e]):
            csum[e] += sch.inc[i]
            count[i] = csum[e]
            pos[i] = k
    with nc.Block() as blk:
        def make_runner(ename):
            def run(h):
                last = {e: 0 for e in ENGS}
                for k, i in enumerate(order[ename]):
                    hw = hoists.get((ename, k))
                    if hw:
                        for de, dop in hw.items():
                            if count[dop] > last[de]:
                                h.wait_ge(sems[de], count[dop])
                                last[de] = count[dop]
                    waits = {}
                    for d in sch.deps[i]:
                        de = sch.eng[d]
                        if de != ename or sch.inc[d] == 16 or sch.inc[i] == 16:
                            if count[d] > waits.get(de, 0):
                                waits[de] = count[d]
                    for e2, cnt in waits.items():
                        if cnt > last[e2]:
                            h.wait_ge(sems[e2], cnt)
                            last[e2] = cnt
                    if sch.delays[i]:
                        h.nop(cycle_cnt=sch.delays[i])
                    inst = sch.fn[i](h)
                    inst.then_inc(sems[ename], sch.inc[i])
                if ename == "sp":
                    h.wait_ge(sems["sp"], csum["sp"])
            return run
        blk.tensor(make_runner("pe"))
        blk.vector(make_runner("dve"))
        blk.scalar(make_runner("act"))
        blk.gpsimd(make_runner("pool"))
        blk.sync(make_runner("sp"))

    return nc


_NC_CACHE = None


def _get_nc():
    global _NC_CACHE
    if _NC_CACHE is None:
        _NC_CACHE = build_nc()
    return _NC_CACHE


def kernel(L0_attn_x, L0_attn_Wq, L0_attn_Wk, L0_attn_Wv, L0_attn_Wg1, L0_attn_Wg2,
           L0_attn_bg2, L0_attn_Wg, L0_attn_g_norm_w, L0_attn_Wo, L0_ffn_weight):
    x = np.ascontiguousarray(np.asarray(L0_attn_x, np.float32))
    base = dict(
        wq=np.ascontiguousarray(np.asarray(L0_attn_Wq, np.float32)),
        wk=np.ascontiguousarray(np.asarray(L0_attn_Wk, np.float32)),
        wv=np.ascontiguousarray(np.asarray(L0_attn_Wv, np.float32)),
        wg=np.ascontiguousarray(np.asarray(L0_attn_Wg, np.float32)),
        wo=np.ascontiguousarray(np.asarray(L0_attn_Wo, np.float32)),
        wg1=np.ascontiguousarray(np.asarray(L0_attn_Wg1, np.float32)),
        wg2b=np.ascontiguousarray(np.concatenate(
            [np.asarray(L0_attn_Wg2, np.float32),
             np.asarray(L0_attn_bg2, np.float32)[None, :]], axis=0)),
        wffnT=np.ascontiguousarray(np.asarray(L0_ffn_weight, np.float32).T),
        gnw=np.ascontiguousarray(np.broadcast_to(
            np.tile(np.asarray(L0_attn_g_norm_w, np.float32), NH), (128, DM))),
        ui16=np.triu(np.full((128, 128), 1.0 / 16.0, np.float32)).astype(BF),
        sl16=np.tril(np.full((128, 128), 1.0 / 16.0, np.float32), k=-1).astype(BF),
        masku=np.tile(np.triu(np.ones((128, 128), np.float32)), (1, 4)).astype(BF),
        ident=np.eye(128, dtype=np.float32).astype(BF),
        ones16=np.full((128, 1), 1.0 / 16.0, np.float32).astype(BF),
        onesrow=np.ones((1, 128), np.float32),
        cpool=np.stack([np.full(128, v, np.float32) for v in
                        (ONE, EPS5, EPS8, LN8INV, LN127, NLN127, 0.0, 0.0)], axis=1),
    )
    in_maps = []
    for b in range(8):
        m = dict(base)
        m["x"] = np.ascontiguousarray(x[b])
        m["xT"] = np.ascontiguousarray(x[b].T)
        in_maps.append(m)
    nc = _get_nc()
    res = run_bass_kernel_spmd(nc, in_maps, core_ids=list(range(8)))
    return np.stack([r["out"] for r in res.results], axis=0)


# version 26
# speedup vs baseline: 1.2072x; 1.1701x over previous
import math
import numpy as np
import ml_dtypes
import concourse.bass as bass
import concourse.mybir as mybir
from concourse.bass_utils import run_bass_kernel_spmd

F32 = mybir.dt.float32
BF16 = mybir.dt.bfloat16
I32 = mybir.dt.int32
AF = mybir.ActivationFunctionType
ALU = mybir.AluOpType
AX = mybir.AxisListType

L = 8192
DM = 1024
C = 128
NCH = L // C
NH = 16
HD = 64

LN8INV = float(np.log(np.float32(0.125)))
LN127 = float(np.log(np.float32(127.0)))
NLN127 = -LN127
ONE = 1.0
EPS5 = 1e-5
EPS8 = 1e-8
WMEANSCALE = float(1.0 / (1024.0 * 1024.0))

BF = ml_dtypes.bfloat16

ENGS = ("pe", "dve", "act", "pool", "sp")
PSUM_BANKS = frozenset(["A0", "A1", "B0", "B1", "C0", "C1", "D0", "D1"])
TINY = frozenset(
    ["c3", "c4", "c5", "rs16", "rstd", "ebt", "wst", "brow", "wbc"]
    + ["wacc%d" % j for j in range(8)]
    + ["l1c0_" + n for n in ("negm", "s2c", "rstd")]
    + ["l2c0_" + n for n in ("negm", "s2c", "rstd")]
)
FRESH_BIG = frozenset(["xbf", "gkh", "qtt", "kht", "kbt", "og0", "og1",
                       "aqbf", "scr2", "S_sb", "S_tmp", "S_bd", "gkf", "sgw",
                       "vbf", "qtf", "khf", "ogf", "aqf", "r1f", "x1f", "aqi"])


class Sched:

    def __init__(self):
        self.n = 0
        self.eng = []
        self.fn = []
        self.inc = []
        self.dur = []
        self.deps = []
        self.res = {}
        self.order = None

    def op(self, eng, fn, reads=(), writes=(), inc=1, dur=1000, tiny_safe=False, guard=False):
        oid = self.n
        self.n += 1
        is_dma = inc == 16
        deps = set()
        for r in reads:
            rec = self.res.get(r)
            if rec and rec["w"] is not None:
                deps.add(rec["w"])
            if rec and r in PSUM_BANKS and eng in ("dve", "act"):
                other = "act" if eng == "dve" else "dve"
                if other in rec["r"]:
                    deps.add(rec["r"][other])
        for r in writes:
            rec = self.res.get(r)
            if rec:
                if rec["w"] is not None:
                    deps.add(rec["w"])
                deps.update(rec["r"].values())
        tdeps = []
        fdeps = []
        if not tiny_safe:
            for r in reads:
                rec = self.res.get(r)
                if rec and rec["w"] is not None:
                    if r in TINY:
                        tdeps.append(rec["w"])
                    elif r in FRESH_BIG:
                        fdeps.append(rec["w"])
        delay = 0
        for r in reads:
            rec = self.res.setdefault(r, {"w": None, "r": {}})
            rec["r"][eng] = oid
        for r in writes:
            self.res[r] = {"w": oid, "r": {}}
        self.eng.append(eng)
        self.fn.append(fn)
        self.inc.append(inc)
        self.dur.append(dur)
        self.deps.append(deps)
        if not hasattr(self, "tinydeps"):
            self.tinydeps = []
            self.freshdeps = []
        self.tinydeps.append(tdeps)
        self.freshdeps.append(fdeps)

    def schedule(self):
        import heapq
        n = self.n
        nsucc = [[] for _ in range(n)]
        indeg = [0] * n
        for i in range(n):
            for d in self.deps[i]:
                nsucc[d].append(i)
                indeg[i] += 1
        cp = [0.0] * n
        for i in range(n - 1, -1, -1):
            m = max((cp[j] for j in nsucc[i]), default=0.0)
            cp[i] = self.dur[i] + m
        self.cp = cp
        fin = [0.0] * n
        start = [0.0] * n
        t_eng = {e: 0.0 for e in ENGS}
        ready = {e: [] for e in ENGS}
        sched_fin = [None] * n
        for i in range(n):
            if indeg[i] == 0:
                heapq.heappush(ready[self.eng[i]], i)
        out = {e: [] for e in ENGS}
        self.dispatch = []
        done = 0
        while done < n:
            best = None
            for e in ENGS:
                if not ready[e]:
                    continue
                cands = heapq.nsmallest(40, ready[e])
                for i in cands:
                    st = max(t_eng[e], max((fin[d] for d in self.deps[i]), default=0.0))
                    key = (st, i)
                    if best is None or key < best[0]:
                        best = (key, e, i)
            _, e, i = best
            ready[e].remove(i)
            heapq.heapify(ready[e])
            st = max(t_eng[e], max((fin[d] for d in self.deps[i]), default=0.0))
            start[i] = st
            fin[i] = st + self.dur[i]
            t_eng[e] = fin[i]
            out[e].append(i)
            self.dispatch.append(i)
            done += 1
            for j in nsucc[i]:
                indeg[j] -= 1
                if indeg[j] == 0:
                    heapq.heappush(ready[self.eng[j]], j)
        self.order = out
        self.start = start
        self.finv = fin
        self.makespan = max(t_eng.values())
        return out

def build_nc(nch=NCH, do_init_ffn=True, do_loop=True, op_limit=None, dump=False):
    nc = bass.Bass("TRN2", target_bir_lowering=False)

    x_d = nc.dram_tensor("x", [L, DM], F32, kind="ExternalInput").ap()
    xT_d = nc.dram_tensor("xT", [DM, L], F32, kind="ExternalInput").ap()
    wq_d = nc.dram_tensor("wq", [DM, DM], F32, kind="ExternalInput").ap()
    wk_d = nc.dram_tensor("wk", [DM, DM], F32, kind="ExternalInput").ap()
    wv_d = nc.dram_tensor("wv", [DM, DM], F32, kind="ExternalInput").ap()
    wg_d = nc.dram_tensor("wg", [DM, DM], F32, kind="ExternalInput").ap()
    wo_d = nc.dram_tensor("wo", [DM, DM], F32, kind="ExternalInput").ap()
    wg1_d = nc.dram_tensor("wg1", [DM, 16], F32, kind="ExternalInput").ap()
    wg2b_d = nc.dram_tensor("wg2b", [17, DM], F32, kind="ExternalInput").ap()
    wffnT_d = nc.dram_tensor("wffnT", [DM, DM], F32, kind="ExternalInput").ap()
    gnw_d = nc.dram_tensor("gnw", [128, DM], F32, kind="ExternalInput").ap()
    ui16_d = nc.dram_tensor("ui16", [128, 128], BF16, kind="ExternalInput").ap()
    sl16_d = nc.dram_tensor("sl16", [128, 128], BF16, kind="ExternalInput").ap()
    masku_d = nc.dram_tensor("masku", [128, 512], BF16, kind="ExternalInput").ap()
    ident_d = nc.dram_tensor("ident", [128, 128], BF16, kind="ExternalInput").ap()
    ones16_d = nc.dram_tensor("ones16", [128, 1], BF16, kind="ExternalInput").ap()
    onesrow_d = nc.dram_tensor("onesrow", [1, 128], F32, kind="ExternalInput").ap()
    cpool_d = nc.dram_tensor("cpool", [128, 8], F32, kind="ExternalInput").ap()
    ws_scr = nc.dram_tensor("ws_scr", [128, 8], F32).ap()
    bc_scr = nc.dram_tensor("bc_scr", [1, 4], F32).ap()
    out_d = nc.dram_tensor("out", [L, DM], F32, kind="ExternalOutput").ap()

    A = nc.alloc_sbuf_tensor
    w_q = A("w_q", [128, 8192], BF16)
    w_k = A("w_k", [128, 8192], BF16)
    w_v = A("w_v", [128, 8192], BF16)
    w_g = A("w_g", [128, 8192], BF16)
    w_o = A("w_o", [128, 8192], BF16)
    w_fT = A("w_fT", [128, 8192], BF16)
    w_g1 = A("w_g1", [128, 128], BF16)
    w_g2b = A("w_g2b", [17, 1024], BF16)
    ui16 = A("sb_ui16", [128, 128], BF16)
    sl16 = A("sb_sl16", [128, 128], BF16)
    masku = A("sb_masku", [128, 512], BF16)
    ident = A("sb_ident", [128, 128], BF16)
    ones16 = A("sb_ones16", [128, 1], BF16)
    onesrow = A("sb_onesrow", [1, 128], F32)
    onescol = A("onescol", [128, 1], F32)
    cpool = A("sb_cpool", [128, 8], F32)
    gnw = A("sb_gnw", [128, DM], F32)

    xtok = [A("xtok0", [128, DM], F32), A("xtok1", [128, DM], F32)]
    outb = [A("outb0", [128, DM], F32), A("outb1", [128, DM], F32)]
    xfeat = [A("xfeat0", [128, DM], BF16), A("xfeat1", [128, DM], BF16)]
    xbf = A("xbf", [128, DM], BF16)
    t1b = A("t1b", [32, 128], BF16)

    scr1 = A("scr1", [128, DM], F32)
    scr2 = A("scr2", [128, DM], F32)
    gkf = A("gkf", [128, DM], F32)
    expB = A("expB", [128, DM], F32)
    expnB = A("expnB", [128, DM], F32)
    expBrev = A("expBrev", [128, DM], F32)
    sq = A("sq", [128, DM], F32)
    r1f = A("r1f", [128, DM], F32)
    x1f = A("x1f", [128, DM], F32)

    gkh = A("gkh", [128, DM], BF16)
    qtt = A("qtt", [128, DM], BF16)
    kht = A("kht", [128, DM], BF16)
    kbt = A("kbt", [128, DM], BF16)
    qtf = A("qtf", [128, DM], BF16)
    khf = A("khf", [128, DM], BF16)
    vbf = A("vbf", [128, DM], BF16)
    og = [A("og0", [128, DM], BF16), A("og1", [128, DM], BF16)]
    ogf = A("ogf", [128, DM], BF16)
    aqbf = A("aqbf", [128, DM], BF16)
    aqf = A("aqf", [128, DM], BF16)
    aqi = A("aqi", [128, DM], I32)
    pm = [A("pm0", [128, 512], BF16), A("pm1", [128, 512], BF16)]

    S_sb = A("S_sb", [128, 512], F32)
    S_tmp = A("S_tmp", [128, 512], F32)
    S_bd = A("S_bd", [128, DM], BF16)
    ebt = A("ebt", [128, 8], F32)

    rs16 = A("rs16", [128, 16], F32)
    rstd = A("rstd", [128, 16], F32)
    cst1 = A("cst1", [128, 16], F32)
    cst2 = A("cst2", [128, 16], F32)
    c3 = A("c3", [128, 1], F32)
    c4 = A("c4", [128, 1], F32)
    c5 = A("c5", [128, 1], F32)

    wst = A("wst", [1, 1], F32)
    wrow = A("wrow", [1, 1024], F32)
    brow = A("brow", [1, 4], F32)
    wbc = A("wbc", [128, 4], F32)
    wacc = A("wacc", [128, 8], F32)
    wsum = A("wsum", [128, 1], F32)

    nc.const_aps.register(F32, ONE, cpool[:, 0:1])
    nc.const_aps.register(F32, EPS5, cpool[:, 1:2])
    nc.const_aps.register(F32, EPS8, cpool[:, 2:3])
    nc.const_aps.register(F32, LN8INV, cpool[:, 3:4])
    nc.const_aps.register(F32, LN127, cpool[:, 4:5])
    nc.const_aps.register(F32, NLN127, cpool[:, 5:6])

    pA = nc.alloc_psum_tensor("pA", [128, 1024], F32)
    pB = nc.alloc_psum_tensor("pB", [128, 1024], F32)
    pC = nc.alloc_psum_tensor("pC", [128, 1024], F32)
    pD = nc.alloc_psum_tensor("pD", [128, 1024], F32)
    pB16 = pB[:].bitcast(BF16)

    sems = {e: nc.alloc_semaphore("sem_" + e) for e in ENGS}
    sch = Sched()

    def proj_mm(psum, lhs_src, w_sb, tag):
        if not callable(lhs_src):
            ap = lhs_src
            lhs_src = lambda k: ap[:, 128 * k:128 * (k + 1)]
        def f(h):
            last = None
            for half in range(2):
                o = psum[:, 512 * half:512 * (half + 1)]
                for k in range(8):
                    last = h.matmul(out=o, lhsT=lhs_src(k),
                                    rhs=w_sb[:, 1024 * k + 512 * half:1024 * k + 512 * half + 512],
                                    start=(k == 0), stop=(k == 7))
            return last
        return f

    def transpose8(psum16, coloff, src, n=8):
        def f(h):
            last = None
            for j in range(n):
                last = h.transpose(out=psum16[:, coloff + 128 * j:coloff + 128 * (j + 1)],
                                   in_=src[:, 128 * j:128 * (j + 1)], identity=ident[:])
            return last
        return f

    def v3(ap, g=16):
        return ap.rearrange("p (g f) -> p g f", g=g)

    def bcast(ap2d, g, f):
        return ap2d.rearrange("p (g o) -> p g o", o=1).broadcast_to([128, g, f])

    def emit_ln(src, srcname, junk, junkname, sqb, sqname, cst, outap, outname, pref):
        negm, s2c, rstd = cst[:, 1:2], cst[:, 2:3], cst[:, 3:4]
        P = pref
        sch.op("act", lambda h: h.activation(out=junk, in_=src, func=AF.Identity,
                                             scale=-1.0 / 1024.0, accum_out=negm),
               reads=[srcname], writes=[junkname, P + "negm"], dur=700)
        sch.op("act", lambda h: h.activation(out=sqb, in_=src, func=AF.Square, bias=negm,
                                             accum_out=s2c),
               reads=[srcname, P + "negm"], writes=[sqname, P + "s2c"], dur=700)
        sch.op("act", lambda h: h.activation(out=rstd, in_=s2c, func=AF.Ln,
                                             scale=1.0 / 1024.0, bias=EPS5),
               reads=[P + "s2c", "cst"], writes=[P + "rstd"], dur=250)
        sch.op("act", lambda h: h.activation(out=rstd, in_=rstd, func=AF.Exp, scale=-0.5),
               reads=[P + "rstd"], writes=[P + "rstd"], dur=250, tiny_safe=True)
        sch.op("dve", lambda h: h.tensor_scalar(out=outap, in0=src, scalar1=negm,
                                                scalar2=rstd, op0=ALU.add, op1=ALU.mult),
               reads=[srcname, P + "negm", P + "rstd"], writes=[outname], dur=700)

    for dst, src in [(ui16, ui16_d), (sl16, sl16_d), (masku, masku_d),
                     (ident, ident_d), (ones16, ones16_d),
                     (onesrow[0:1, :], onesrow_d), (cpool, cpool_d), (gnw, gnw_d)]:
        sch.op("sp", (lambda d, s: (lambda h: h.dma_start(out=d, in_=s)))(
            dst[:] if hasattr(dst, "ap") else dst, src),
            writes=["cst"], inc=16)
    for name, dst, src, a in [("w_q", w_q, wq_d, 8), ("w_k", w_k, wk_d, 8),
                              ("w_v", w_v, wv_d, 8), ("w_g", w_g, wg_d, 8),
                              ("w_o", w_o, wo_d, 8)]:
        dview = dst[:].rearrange("p (a n) -> p a n", a=a)
        sview = src.rearrange("(a p) n -> p a n", p=128)
        sch.op("pool", (lambda d, s: (lambda h: h.dma_start(out=d, in_=s)))(dview, sview),
               writes=[name], inc=16)
    sch.op("pool", lambda h: h.dma_start(
        out=w_g1[:].rearrange("p (a n) -> p a n", n=16),
        in_=wg1_d.rearrange("(a p) n -> p a n", p=128)), writes=["w_g1"], inc=16)
    sch.op("pool", lambda h: h.dma_start(out=w_g2b[0:17, :], in_=wg2b_d),
           writes=["w_g2b"], inc=16)

    sch.op("dve", lambda h: h.memset(onescol[:], 1.0), writes=["onescol"])
    sch.op("dve", lambda h: h.memset(t1b[:], 1.0), writes=["t1b_ones"])
    sch.op("dve", lambda h: h.memset(S_sb[:], 0.0), writes=["S_sb"])
    sch.op("dve", lambda h: h.memset(S_bd[:], 0.0), writes=["S_bd"])

    if not do_init_ffn:
        wfT_all = []

    for j in range(8 if do_init_ffn else 0):
        buf, rname = (scr1, "scr1") if j % 2 == 0 else (scr2, "scr2")
        sch.op("sp", (lambda b, jj: (lambda h: h.dma_start(out=b[:], in_=wffnT_d[128 * jj:128 * (jj + 1), :])))(buf, j),
               writes=[rname], inc=16)
        sch.op("dve", (lambda b, jj: (lambda h: h.reduce_sum(
            out=wacc[:, jj:jj + 1], in_=b[:], axis=AX.X, apply_absolute_value=True)))(buf, j),
            reads=[rname], writes=["wacc%d" % j])
    if do_init_ffn:
      sch.op("act", lambda h: h.copy(out=ebt[:], in_=wacc[:]),
           reads=["wacc%d" % j for j in range(8)], writes=["ebt"])
    if do_init_ffn:
      def f_wsscr(h):
          h.nop(cycle_cnt=4000)
          return h.dma_start(out=ws_scr, in_=ebt[:])
      sch.op("sp", f_wsscr, reads=["ebt"], writes=["ws_scr"], inc=16)
      sch.op("sp", lambda h: h.dma_start(
          out=wrow[0:1, :], in_=ws_scr.rearrange("(o a) b -> o (a b)", o=1)),
          reads=["ws_scr"], writes=["wrow"], inc=16)
      sch.op("act", lambda h: h.activation(out=wrow[0:1, :], in_=wrow[0:1, :],
                                           func=AF.Identity, accum_out=wst[0:1, :]),
             reads=["wrow"], writes=["wrow", "wst"])
      sch.op("act", lambda h: h.activation(out=wst[0:1, :], in_=wst[0:1, :], func=AF.Ln,
                                           scale=WMEANSCALE, bias=EPS8),
             reads=["wst", "cst"], writes=["wst"])
      sch.op("act", lambda h: h.activation(out=brow[0:1, 0:1], in_=wst[0:1, :],
                                           func=AF.Exp, scale=-1.0),
             reads=["wst"], writes=["brow"])
      sch.op("act", lambda h: h.activation(out=brow[0:1, 1:2], in_=wst[0:1, :],
                                           func=AF.Exp, bias=NLN127),
             reads=["wst", "cst"], writes=["brow"])
      sch.op("act", lambda h: h.activation(out=brow[0:1, 2:3], in_=brow[0:1, 1:2],
                                           func=AF.Identity, scale=EPS8),
             reads=["brow"], writes=["brow"])
      def f_bcscr(h):
          h.nop(cycle_cnt=4000)
          return h.dma_start(out=bc_scr, in_=brow[0:1, :])
      sch.op("sp", f_bcscr, reads=["brow"], writes=["bc_scr"], inc=16)
      sch.op("sp", lambda h: h.dma_start(out=wbc[:], in_=bc_scr.broadcast_to([128, 4])),
             reads=["bc_scr"], writes=["wbc"], inc=16)
    for j in ([0, 1, 2, 3, 4, 5, 6, 7, 0] if do_init_ffn else []):
        sch.op("sp", (lambda jj: (lambda h: h.dma_start(out=scr1[:], in_=wffnT_d[128 * jj:128 * (jj + 1), :])))(j),
               writes=["scr1"], inc=16)
        sch.op("dve", lambda h: h.tensor_scalar(out=aqi[:], in0=scr1[:], scalar1=wbc[:, 0:1],
                                                scalar2=None, op0=ALU.mult),
               reads=["scr1", "wbc"], writes=["aqi"])
        def f_clip(h):
            h.nop(cycle_cnt=4000)
            return h.tensor_scalar(out=scr2[:], in0=aqi[:], scalar1=1.0,
                                   scalar2=-1.0, op0=ALU.min, op1=ALU.max)
        sch.op("dve", f_clip, reads=["aqi"], writes=["scr2"])
        def f_wft(h, jj=j):
            h.nop(cycle_cnt=4000)
            return h.copy(out=w_fT[:, 1024 * jj:1024 * (jj + 1)], in_=scr2[:])
        sch.op("act", f_wft, reads=["scr2"], writes=["w_fT%d" % j])

    if do_init_ffn:
        wfT_all = ["w_fT%d" % j for j in range(8)]

    init_gate = (["w_fT%d" % j for j in range(8)] + ["wbc", "onescol", "t1b_ones",
                 "S_sb", "S_bd", "cst"] + list(wfT_all)) if do_init_ffn else ["cst"]
    for e in ("pe", "dve", "act", "pool"):
        sch.op(e, (lambda: (lambda h: h.nop(cycle_cnt=8000)))(),
               reads=init_gate, writes=["bar_" + e], dur=7000)
    init_gate = init_gate + ["bar_pe", "bar_dve", "bar_act", "bar_pool"]
    for c0 in range(min(2, nch if do_loop else 0)):
        def f_prime(h, cc=c0):
            h.nop(cycle_cnt=4000)
            return h.dma_start(out=xtok[cc % 2][:], in_=x_d[128 * cc:128 * (cc + 1), :])
        sch.op("sp", f_prime, reads=init_gate,
               writes=["xtok%d" % (c0 % 2)], inc=16)

    sch.limit = op_limit
    pC16 = pC[:].bitcast(BF16)
    pD16 = pD[:].bitcast(BF16)

    def emit_head_mid(c):
        guard = c < 3
        _schop = sch.op
        def sopg(*a, **kw):
            kw.setdefault("guard", guard)
            return _schop(*a, **kw)
        sch_op_saved = sch.op
        p = c % 2
        XT = "xtok%d" % p
        XF = "xfeat%d" % p
        OG = "og%d" % p
        xfb = xfeat[p]

        def xf_slice(k):
            return xfb[:, 128 * k:128 * (k + 1)]

        sopg("pool", (lambda pp: (lambda h: h.tensor_copy(out=xbf[:], in_=xtok[pp][:])))(p),
               reads=[XT], writes=["xbf"], dur=1200)
        sopg("pe", transpose8(pC16, 0, xbf[:]), reads=["xbf", "cst"], writes=["C0"], dur=2600)
        sopg("act", (lambda pp: (lambda h: h.copy(out=xfeat[pp][:], in_=pC16[:, 0:1024])))(p),
               reads=["C0"], writes=[XF], dur=420)

        def f_t1(h):
            last = None
            for k in range(8):
                last = h.matmul(out=pC[0:16, 0:128], lhsT=w_g1[:, 16 * k:16 * (k + 1)],
                                rhs=xf_slice(k),
                                start=(k == 0), stop=(k == 7))
            return last
        sopg("pe", f_t1, reads=[XF, "w_g1"], writes=["C0"], dur=800)
        sopg("act", lambda h: h.copy(out=t1b[0:16, :], in_=pC[0:16, 0:128]),
               reads=["C0"], writes=["t1b"], dur=300)

        def f_z(h):
            last = None
            for half in range(2):
                last = h.matmul(out=pC[:, 512 * half:512 * (half + 1)], lhsT=t1b[0:17, :],
                                rhs=w_g2b[0:17, 512 * half:512 * (half + 1)],
                                start=True, stop=True)
            return last
        sopg("pe", f_z, reads=["t1b", "t1b_ones", "w_g2b"], writes=["C0", "C1"], dur=500)

        sopg("act", lambda h: h.activation(out=scr1[:], in_=pC[:], func=AF.Abs),
               reads=["C0", "C1"], writes=["scr1"], dur=620)
        sopg("act", lambda h: h.activation(out=scr2[:], in_=scr1[:], func=AF.Exp, scale=-1.0),
               reads=["scr1"], writes=["scr2"], dur=620)
        sopg("act", lambda h: h.activation(out=scr1[:], in_=scr2[:], func=AF.Ln, bias=ONE),
               reads=["scr2", "cst"], writes=["scr1"], dur=620)
        sopg("dve", lambda h: h.tensor_scalar(out=scr2[:], in0=pC[:], scalar1=0.0,
                                                scalar2=None, op0=ALU.min),
               reads=["C0", "C1"], writes=["scr2"], dur=660)
        sopg("pool", lambda h: h.tensor_tensor(out=gkh[:], in0=scr2[:], in1=scr1[:], op=ALU.subtract),
               reads=["scr1", "scr2"], writes=["gkh"], dur=2500)

        def f_btot(h):
            last = None
            for j in range(8):
                last = h.matmul(out=pD[:, j:j + 1], lhsT=gkh[:, 128 * j:128 * (j + 1)],
                                rhs=ones16[:], start=True, stop=True)
            return last
        sopg("pe", f_btot, reads=["gkh", "cst"], writes=["D0"], dur=700)
        sopg("act", lambda h: h.activation(out=ebt[:], in_=pD[:, 0:8], func=AF.Exp),
               reads=["D0"], writes=["ebt"], dur=250)

        def f_B(h):
            last = None
            for half in range(2):
                last = h.matmul(out=pD[:, 512 * half:512 * (half + 1)], lhsT=ui16[:],
                                rhs=gkh[:, 512 * half:512 * half + 512], start=True, stop=True)
            return last
        sopg("pe", f_B, reads=["gkh", "cst"], writes=["D0", "D1"], dur=550)

        sopg("act", lambda h: h.activation(out=expB[:], in_=pD[:], func=AF.Exp, bias=LN8INV),
               reads=["D0", "D1", "cst"], writes=["expB"], dur=620)
        sopg("act", lambda h: h.activation(out=expnB[:], in_=pD[:], func=AF.Exp, scale=-1.0),
               reads=["D0", "D1"], writes=["expnB"], dur=620)

        sopg("pe", proj_mm(pC, xf_slice, w_g, "g"), reads=[XF, "w_g"], writes=["C0", "C1"], dur=3900)
        sopg("act", lambda h: h.activation(out=scr1[:], in_=pC[:], func=AF.Abs),
               reads=["C0", "C1"], writes=["scr1"], dur=620)
        sopg("act", lambda h: h.activation(out=scr1[:], in_=scr1[:], func=AF.Exp, scale=-1.0),
               reads=["scr1"], writes=["scr1"], dur=620)
        sopg("act", lambda h: h.activation(out=scr1[:], in_=scr1[:], func=AF.Ln, bias=ONE),
               reads=["scr1", "cst"], writes=["scr1"], dur=620)
        sopg("dve", lambda h: h.tensor_scalar(out=scr2[:], in0=pC[:], scalar1=0.0,
                                                scalar2=None, op0=ALU.min),
               reads=["C0", "C1"], writes=["scr2"], dur=660)
        sopg("dve", lambda h: h.tensor_tensor(out=scr2[:], in0=scr2[:], in1=scr1[:], op=ALU.subtract),
               reads=["scr1", "scr2"], writes=["scr2"], dur=1130)
        sopg("act", lambda h: h.activation(out=gkf[:], in_=scr2[:], func=AF.Exp),
               reads=["scr2"], writes=["gkf"], dur=620)
        sopg("dve", lambda h: h.scalar_tensor_tensor(out=scr1[:], in0=pC[:], scalar=1.0,
                                                       in1=gkf[:], op0=ALU.mult, op1=ALU.mult),
               reads=["C0", "C1", "gkf"], writes=["scr1"], dur=1200)
        sopg("pool", lambda h: h.tensor_tensor(out=scr2[:], in0=scr1[:], in1=gnw[:], op=ALU.mult),
               reads=["scr1", "cst"], writes=["scr2"], dur=2800)

        sopg("pe", proj_mm(pD, xf_slice, w_q, "q"), reads=[XF, "w_q"], writes=["D0", "D1"], dur=3900)
        sopg("dve", lambda h: h.tensor_tensor(out=qtt[:], in0=pD[:], in1=expB[:], op=ALU.mult),
               reads=["D0", "D1", "expB"], writes=["qtt"], dur=1200)
        sopg("pe", proj_mm(pD, xf_slice, w_k, "k"), reads=[XF, "w_k"], writes=["D0", "D1"], dur=3900)
        sopg("dve", lambda h: h.tensor_tensor(out=kht[:], in0=pD[:], in1=expnB[:], op=ALU.mult),
               reads=["D0", "D1", "expnB"], writes=["kht"], dur=1200)

        sopg("pe", transpose8(pD16, 0, qtt[:]), reads=["qtt", "cst"], writes=["D0"], dur=2600)
        sopg("pe", transpose8(pD16, 1024, kht[:]), reads=["kht", "cst"], writes=["D1"], dur=2600)
        sopg("dve", lambda h: h.tensor_copy(out=qtf[:], in_=pD16[:, 0:1024]), reads=["D0"], writes=["qtf"], dur=420)
        sopg("dve", lambda h: h.tensor_copy(out=khf[:], in_=pD16[:, 1024:2048]), reads=["D1"], writes=["khf"], dur=420)

        sopg("pe", proj_mm(pD, xf_slice, w_v, "v"), reads=[XF, "w_v"], writes=["D0", "D1"], dur=3900)
        sopg("act", lambda h: h.copy(out=vbf[:], in_=pD[:]), reads=["D0", "D1"], writes=["vbf"], dur=620)

        for b8 in range(2):
            def f_P8(h, bb=b8):
                last = None
                for i in range(8):
                    hh = 8 * bb + i
                    b = 64 * (hh % 2)
                    j = hh // 2
                    last = h.matmul(out=pC[:, 512 * (hh % 2) + 128 * (j % 4):
                                            512 * (hh % 2) + 128 * (j % 4) + 128],
                                    lhsT=khf[b:b + 64, 128 * j:128 * (j + 1)],
                                    rhs=qtf[b:b + 64, 128 * j:128 * (j + 1)],
                                    start=True, stop=True)
                return last
            sopg("pe", f_P8, reads=["khf", "qtf"], writes=["C0", "C1"], dur=1100)
            sopg("dve", lambda h: h.tensor_tensor(
                out=pm[0][:], in0=pC[:, 0:512], in1=masku[:], op=ALU.mult),
                reads=["C0", "cst"], writes=["pm0"], dur=660)
            sopg("dve", lambda h: h.tensor_tensor(
                out=pm[1][:], in0=pC[:, 512:1024], in1=masku[:], op=ALU.mult),
                reads=["C1", "cst"], writes=["pm1"], dur=660)

            for jp in range(4 * b8, 4 * b8 + 4):
                def f_O(h, j=jp):
                    last = h.matmul(out=pD[:, 128 * j:128 * (j + 1)],
                                    lhsT=qtf[:, 128 * j:128 * (j + 1)],
                                    rhs=S_bd[:, 128 * j:128 * (j + 1)],
                                    start=True, stop=False)
                    for t in range(2):
                        hh = 2 * j + t
                        last = h.matmul(out=pD[:, 64 * hh:64 * (hh + 1)],
                                        lhsT=pm[t][:, 128 * (j % 4):128 * (j % 4) + 128],
                                        rhs=vbf[:, 64 * hh:64 * (hh + 1)],
                                        start=False, stop=(t == 1))
                    return last
                sopg("pe", f_O, reads=["qtf", "S_bd", "pm0", "pm1", "vbf"],
                       writes=["D%d" % (jp // 4)], dur=400)

        def f_S(h, h0, h1):
            last = None
            for hh in range(h0, h1):
                b = 64 * (hh % 2)
                j = hh // 2
                last = h.matmul(out=pC[b:b + 64, 64 * j:64 * (j + 1)],
                                lhsT=kht[:, 64 * hh:64 * (hh + 1)],
                                rhs=vbf[:, 64 * hh:64 * (hh + 1)],
                                start=True, stop=True)
            return last
        sopg("pe", (lambda fS: (lambda h: fS(h, 0, 8)))(f_S), reads=["kht", "vbf"], writes=["C0"], dur=900)
        sopg("pe", (lambda fS: (lambda h: fS(h, 8, 16)))(f_S), reads=["kht", "vbf"], writes=["C0"], dur=900)

        sopg("dve", lambda h: h.tensor_tensor(out=S_tmp[:], in0=pC[:, 0:512], in1=S_sb[:], op=ALU.add),
               reads=["C0", "S_sb"], writes=["S_tmp"], dur=660)
        sopg("dve", lambda h: h.tensor_tensor(out=v3(S_sb[:], 8), in0=v3(S_tmp[:], 8),
                                                in1=bcast(ebt[:], 8, 64), op=ALU.mult),
               reads=["S_tmp", "ebt"], writes=["S_sb"], dur=620)
        sopg("act", lambda h: h.copy(
            out=S_bd[0:64, :].rearrange("p (j n) -> p j n", j=8)[:, :, 0:64],
            in_=v3(S_sb[0:64, :], 8)), reads=["S_sb"], writes=["S_bd"], dur=450)
        sopg("act", lambda h: h.copy(
            out=S_bd[64:128, :].rearrange("p (j n) -> p j n", j=8)[:, :, 64:128],
            in_=v3(S_sb[64:128, :], 8)), reads=["S_sb"], writes=["S_bd"], dur=450)

        sopg("act", lambda h: h.activation(out=sq[:], in_=pD[:], func=AF.Square),
               reads=["D0", "D1"], writes=["sq"], dur=620)
        sopg("dve", lambda h: h.reduce_sum(out=rs16[:], in_=v3(sq[:], 16), axis=AX.X),
               reads=["sq"], writes=["rs16"], dur=1130)
        sopg("act", lambda h: h.activation(out=rs16[:], in_=rs16[:], func=AF.Ln,
                                             scale=1.0 / 64.0, bias=EPS5),
               reads=["rs16", "cst"], writes=["rs16"], dur=250)
        sopg("act", lambda h: h.activation(out=rstd[:], in_=rs16[:], func=AF.Exp, scale=-0.5),
               reads=["rs16"], writes=["rstd"], dur=250, tiny_safe=True)
        sopg("dve", lambda h: h.tensor_tensor(out=v3(gkf[:], 16), in0=v3(pD[:], 16),
                                                in1=bcast(rstd[:], 16, 64), op=ALU.mult),
               reads=["D0", "D1", "rstd"], writes=["gkf"], dur=1200)
        sopg("dve", (lambda pp: (lambda h: h.tensor_tensor(
            out=og[pp][:], in0=gkf[:], in1=scr2[:], op=ALU.mult)))(p),
            reads=["gkf", "scr2"], writes=[OG], dur=1130)

    def emit_tail(c):
        guard = c < 3
        _schop = sch.op
        def sopg(*a, **kw):
            kw.setdefault("guard", guard)
            return _schop(*a, **kw)
        p = c % 2
        XT = "xtok%d" % p
        OG = "og%d" % p

        sopg("pe", transpose8(pB16, 0, og[p][:]), reads=[OG, "cst"], writes=["B0"], dur=2600)
        sopg("dve", lambda h: h.tensor_copy(out=ogf[:], in_=pB16[:, 0:1024]), reads=["B0"], writes=["ogf"], dur=420)
        sopg("pe", proj_mm(pA, ogf[:], w_o, "wo"), reads=["ogf", "w_o"], writes=["A0", "A1"], dur=3900)
        sopg("dve", (lambda pp: (lambda h: h.tensor_tensor(
            out=r1f[:], in0=pA[:], in1=xtok[pp][:], op=ALU.add)))(p),
            reads=["A0", "A1", XT], writes=["r1f"], dur=1200)

        if dump and c == 0:
            sopg("pool", lambda h: h.dma_start(out=out_d[4096:4224, :], in_=r1f[:]),
                   reads=["r1f"], writes=["dumpr1"], inc=16)
        if c + 2 < nch:
            sopg("sp", (lambda cc: (lambda h: h.dma_start(
                out=xtok[cc % 2][:], in_=x_d[128 * cc:128 * (cc + 1), :])))(c + 2),
                writes=[XT], inc=16, dur=3500)

        emit_ln(r1f[:], "r1f", sq[:], "sq", sq[:], "sq", cst1, x1f[:], "x1f", "l1c0_")

        sopg("dve", lambda h: h.reduce_max(out=c3[:], in_=x1f[:], axis=AX.X,
                                             apply_absolute_value=True),
               reads=["x1f"], writes=["c3"], dur=1130)
        sopg("act", lambda h: h.activation(out=c4[:], in_=c3[:], func=AF.Ln, bias=EPS8),
               reads=["c3", "cst"], writes=["c4"], dur=250)
        sopg("act", lambda h: h.activation(out=c4[:], in_=c4[:], func=AF.Exp, scale=-1.0, bias=LN127),
               reads=["c4", "cst"], writes=["c4"], dur=250, tiny_safe=True)
        sopg("act", lambda h: h.activation(out=c5[:], in_=c3[:], func=AF.Identity,
                                             scale=wbc[:, 1:2], bias=wbc[:, 2:3]),
               reads=["c3", "wbc"], writes=["c5"], dur=250)
        sopg("dve", lambda h: h.tensor_scalar(out=aqi[:], in0=x1f[:], scalar1=c4[:, 0:1],
                                                scalar2=None, op0=ALU.mult),
               reads=["x1f", "c4"], writes=["aqi"], dur=660)
        sopg("dve", lambda h: h.tensor_copy(out=aqbf[:], in_=aqi[:]), reads=["aqi"], writes=["aqbf"], dur=1130)
        sopg("pe", transpose8(pB16, 1024, aqbf[:]), reads=["aqbf", "cst"], writes=["B1"], dur=2600)
        sopg("dve", lambda h: h.tensor_copy(out=aqf[:], in_=pB16[:, 1024:2048]), reads=["B1"], writes=["aqf"], dur=420)
        sopg("pe", proj_mm(pA, aqf[:], w_fT, "ffn"), reads=["aqf"] + wfT_all, writes=["A0", "A1"], dur=3900)
        sopg("dve", lambda h: h.scalar_tensor_tensor(out=r1f[:], in0=pA[:], scalar=c5[:, 0:1],
                                                       in1=x1f[:], op0=ALU.mult, op1=ALU.add),
               reads=["A0", "A1", "c5", "x1f"], writes=["r1f"], dur=1200)

        emit_ln(r1f[:], "r1f", sq[:], "sq", sq[:], "sq", cst2,
                outb[p][:], "outb%d" % p, "l2c0_")
        def f_store(h, cc=c, pp=p):
            h.nop(cycle_cnt=2500)
            return h.dma_start(out=out_d[128 * cc:128 * (cc + 1), :], in_=outb[pp][:])
        sopg("sp", f_store, reads=["outb%d" % p], inc=16, dur=5500)

    for c in range(nch if do_loop else 0):
        emit_head_mid(c)
        if c >= 1:
            emit_tail(c - 1)
    if do_loop and nch > 0:
        emit_tail(nch - 1)

    if dump:
        sch.limit = None
        dumps = [("gkh", gkh, BF16), ("expB", expB, F32), ("expnB", expnB, F32),
                 ("expBrev", expBrev, F32), ("qtt", qtt, BF16), ("kht", kht, BF16),
                 ("kbt", kbt, BF16), ("vbf", vbf, BF16), ("qtf", qtf, BF16),
                 ("khf", khf, BF16), ("og", og[0], BF16), ("ogf", ogf, BF16),
                 ("r1f", r1f, F32), ("x1f", x1f, F32), ("aqbf", aqbf, BF16),
                 ("S_sb", S_sb, F32), ("ebt", ebt, F32),
                 ("pm0", pm[0], BF16), ("pm1", pm[1], BF16), ("scr2gate", scr2, F32),
                 ("wbc", wbc, F32), ("aqf", aqf, BF16),
                 ("wacc", wacc, F32), ("wsum", wsum, F32), ("cst1", cst1, F32),
                 ("cst2", cst2, F32)]
        for jj in range(8):
            sch.op("pool", (lambda j2: (lambda h: h.dma_start(
                out=out_d[4352 + 128 * j2:4480 + 128 * j2, :],
                in_=w_fT[:, 1024 * j2:1024 * (j2 + 1)])))(jj),
                reads=["w_fT%d" % jj], writes=["dumpwft%d" % jj], inc=16)
        for di, (rname, buf, dt) in enumerate(dumps):
            row = 128 * (8 + di)
            wdt = buf[:].shape[1]
            sch.op("pool", (lambda b, r, w: (lambda h: h.dma_start(
                out=out_d[r:r + 128, 0:w], in_=b[:],
                allow_slow_non_contiguous=True)))(buf, row, wdt),
                reads=[rname], writes=["dump%d" % di], inc=16)
        globals()["DUMP_ROWS"] = {rname: 128 * (8 + di) for di, (rname, _, _) in enumerate(dumps)}

    base_dur = list(sch.dur)
    order = sch.schedule()
    def place_protection():
        sch.delays = [0] * sch.n
        for i in range(sch.n):
            for d in sch.tinydeps[i]:
                if sch.start[i] - sch.finv[d] < 4000.0:
                    sch.delays[i] = 1500
            for d in sch.freshdeps[i]:
                if sch.start[i] - sch.finv[d] < 1800.0:
                    sch.delays[i] = max(sch.delays[i], 1500)
        return {}

    hoists = place_protection()
    for _ in range(2):
        sch.dur = [base_dur[i] + sch.delays[i] * 1.25 for i in range(sch.n)]
        order = sch.schedule()
        hoists = place_protection()
    globals()["LAST_SCHED"] = sch
    count = {}
    csum = {e: 0 for e in ENGS}
    pos = {}
    for e in ENGS:
        for k, i in enumerate(order[e]):
            csum[e] += sch.inc[i]
            count[i] = csum[e]
            pos[i] = k
    with nc.Block() as blk:
        def make_runner(ename):
            def run(h):
                last = {e: 0 for e in ENGS}
                for k, i in enumerate(order[ename]):
                    hw = hoists.get((ename, k))
                    if hw:
                        for de, dop in hw.items():
                            if count[dop] > last[de]:
                                h.wait_ge(sems[de], count[dop])
                                last[de] = count[dop]
                    waits = {}
                    for d in sch.deps[i]:
                        de = sch.eng[d]
                        if de != ename or sch.inc[d] == 16 or sch.inc[i] == 16:
                            if count[d] > waits.get(de, 0):
                                waits[de] = count[d]
                    for e2, cnt in waits.items():
                        if cnt > last[e2]:
                            h.wait_ge(sems[e2], cnt)
                            last[e2] = cnt
                    if sch.delays[i]:
                        h.nop(cycle_cnt=sch.delays[i])
                    inst = sch.fn[i](h)
                    inst.then_inc(sems[ename], sch.inc[i])
                if ename == "sp":
                    h.wait_ge(sems["sp"], csum["sp"])
            return run
        blk.tensor(make_runner("pe"))
        blk.vector(make_runner("dve"))
        blk.scalar(make_runner("act"))
        blk.gpsimd(make_runner("pool"))
        blk.sync(make_runner("sp"))

    return nc


_NC_CACHE = None


def _get_nc():
    global _NC_CACHE
    if _NC_CACHE is None:
        _NC_CACHE = build_nc()
    return _NC_CACHE


def kernel(L0_attn_x, L0_attn_Wq, L0_attn_Wk, L0_attn_Wv, L0_attn_Wg1, L0_attn_Wg2,
           L0_attn_bg2, L0_attn_Wg, L0_attn_g_norm_w, L0_attn_Wo, L0_ffn_weight):
    x = np.ascontiguousarray(np.asarray(L0_attn_x, np.float32))
    base = dict(
        wq=np.ascontiguousarray(np.asarray(L0_attn_Wq, np.float32)),
        wk=np.ascontiguousarray(np.asarray(L0_attn_Wk, np.float32)),
        wv=np.ascontiguousarray(np.asarray(L0_attn_Wv, np.float32)),
        wg=np.ascontiguousarray(np.asarray(L0_attn_Wg, np.float32)),
        wo=np.ascontiguousarray(np.asarray(L0_attn_Wo, np.float32)),
        wg1=np.ascontiguousarray(np.asarray(L0_attn_Wg1, np.float32)),
        wg2b=np.ascontiguousarray(np.concatenate(
            [np.asarray(L0_attn_Wg2, np.float32),
             np.asarray(L0_attn_bg2, np.float32)[None, :]], axis=0)),
        wffnT=np.ascontiguousarray(np.asarray(L0_ffn_weight, np.float32).T),
        gnw=np.ascontiguousarray(np.broadcast_to(
            np.tile(np.asarray(L0_attn_g_norm_w, np.float32), NH), (128, DM))),
        ui16=np.triu(np.full((128, 128), 1.0 / 16.0, np.float32)).astype(BF),
        sl16=np.tril(np.full((128, 128), 1.0 / 16.0, np.float32), k=-1).astype(BF),
        masku=np.tile(np.triu(np.ones((128, 128), np.float32)), (1, 4)).astype(BF),
        ident=np.eye(128, dtype=np.float32).astype(BF),
        ones16=np.full((128, 1), 1.0 / 16.0, np.float32).astype(BF),
        onesrow=np.ones((1, 128), np.float32),
        cpool=np.stack([np.full(128, v, np.float32) for v in
                        (ONE, EPS5, EPS8, LN8INV, LN127, NLN127, 0.0, 0.0)], axis=1),
    )
    in_maps = []
    for b in range(8):
        m = dict(base)
        m["x"] = np.ascontiguousarray(x[b])
        m["xT"] = np.ascontiguousarray(x[b].T)
        in_maps.append(m)
    nc = _get_nc()
    res = run_bass_kernel_spmd(nc, in_maps, core_ids=list(range(8)))
    return np.stack([r["out"] for r in res.results], axis=0)
